# version 01
# baseline (speedup 1.0000x reference)
import os
import numpy as np
import ml_dtypes
from contextlib import ExitStack

import concourse.bass as bass
import concourse.mybir as mybir
from concourse.bass_utils import run_bass_kernel_spmd

F32 = mybir.dt.float32
BF16 = mybir.dt.bfloat16
F8 = mybir.dt.float8e4
AF = mybir.ActivationFunctionType
ALU = mybir.AluOpType

NCORES = 8
B, S, D = 2, 4096, 2048
T = 1024
NCH = 16
NH, DN, DR, DQ, DV, LORA = 16, 128, 64, 192, 128, 512
CA, CB = 25.0, 30.0
NEGINF = -30000.0
SCALE_A = float(192.0 ** -0.5)
SCALE_B = float(128.0 ** -0.5)
LN_EPS = 1e-5
RMS_EPS = 1e-6

STAGE = int(os.environ.get("MK_STAGE", "5"))
DBG = int(os.environ.get("MK_DBG", "0"))

BF = ml_dtypes.bfloat16


class PB:
    ENGINES = ("tensor", "vector", "scalar", "gpsimd", "sync")

    def __init__(self, nc, es):
        self.nc = nc
        self.es = es
        self.ops = {e: [] for e in self.ENGINES}
        self.sems = {}
        self.cnt = {}
        for ch in ("L", "G", "P", "V", "S", "Q", "C"):
            self.sems[ch] = es.enter_context(nc.semaphore("sem" + ch))
            self.cnt[ch] = 0

    def wait(self, eng, ch, val=None):
        v = self.cnt[ch] if val is None else val
        if v <= 0:
            return
        h = self.sems[ch]
        self.ops[eng].append(lambda e, h=h, v=v: e.wait_ge(h, v))

    def do(self, eng, fn, inc=None, n=1):
        if inc is None:
            self.ops[eng].append(fn)
        else:
            h = self.sems[inc]
            self.cnt[inc] += n
            self.ops[eng].append(lambda e, fn=fn, h=h, n=n: fn(e).then_inc(h, n))

    def dma(self, eng, out, in_, ch):
        self.do(eng, lambda e, o=out, i=in_: e.dma_start(out=o, in_=i), inc=ch, n=16)

    def join(self, engines=None, chans=None):
        if engines is None:
            engines = self.ENGINES
        if chans is None:
            chans = ("L", "G", "P", "V", "S", "Q")
        snap = {ch: self.cnt[ch] for ch in chans}
        for eng in engines:
            for ch in chans:
                self.wait(eng, ch, snap[ch])


def build_program():
    nc = bass.Bass("TRN2", target_bir_lowering=False, num_devices=NCORES,
                   dynamic_dma_scratch_size=8192)
    es = ExitStack()

    d_x0 = nc.dram_tensor("x0kvr", [128, NCH * 4096], F32, kind="ExternalInput").ap()
    d_biasA = nc.dram_tensor("biasA", [128, 32], F32, kind="ExternalInput").ap()
    d_stripeB = nc.dram_tensor("stripeB", [128, 8], F32, kind="ExternalInput").ap()
    d_wqtn = nc.dram_tensor("wqtn", [NH, 128, 2048], F32, kind="ExternalInput").ap()
    d_wqpe = nc.dram_tensor("wqpe", [NH, 64, 2048], F32, kind="ExternalInput").ap()
    d_wkvat = nc.dram_tensor("wkvat", [128, NCH * 512], F32, kind="ExternalInput").ap()
    d_wkvape = nc.dram_tensor("wkvape", [64, 2048], F32, kind="ExternalInput").ap()
    d_wkbk = nc.dram_tensor("wkbk", [NH, 128, 512], F32, kind="ExternalInput").ap()
    d_wkbv = nc.dram_tensor("wkbv", [NH, 128, 512], F32, kind="ExternalInput").ap()
    d_wot = nc.dram_tensor("wot", [NCH, 128, 2048], F32, kind="ExternalInput").ap()
    d_wqb = nc.dram_tensor("wqb", [NH, 128, 2048], F32, kind="ExternalInput").ap()
    d_wkb = nc.dram_tensor("wkb", [NH, 128, 2048], F32, kind="ExternalInput").ap()
    d_wvb = nc.dram_tensor("wvb", [NH, 128, 2048], F32, kind="ExternalInput").ap()
    d_gw = nc.dram_tensor("gw", [48, 128, 2048], F32, kind="ExternalInput").ap()
    d_uw = nc.dram_tensor("uw", [48, 128, 2048], F32, kind="ExternalInput").ap()
    d_dw = nc.dram_tensor("dw", [48, 128, 2048], F32, kind="ExternalInput").ap()
    d_kvnw = nc.dram_tensor("kvnwd", [128, 4], F32, kind="ExternalInput").ap()
    d_cos = nc.dram_tensor("cosd", [64, 1], F32, kind="ExternalInput").ap()
    d_sinr = nc.dram_tensor("sinrd", [64, 1], F32, kind="ExternalInput").ap()
    d_signv = nc.dram_tensor("signvd", [64, 1], F32, kind="ExternalInput").ap()
    d_bqt = nc.dram_tensor("bqtd", [128, 16], F32, kind="ExternalInput").ap()
    d_bkt = nc.dram_tensor("bktd", [128, 16], F32, kind="ExternalInput").ap()
    d_bvr = nc.dram_tensor("bvrd", [1, 2048], F32, kind="ExternalInput").ap()
    d_gsc = nc.dram_tensor("gscd", [128, 48], F32, kind="ExternalInput").ap()
    d_usc = nc.dram_tensor("uscd", [128, 48], F32, kind="ExternalInput").ap()
    d_dsc = nc.dram_tensor("dscd", [128, 16], F32, kind="ExternalInput").ap()
    d_qspn = nc.dram_tensor("qspn", [NH, 128, 1024], BF16)
    d_qspr = nc.dram_tensor("qspr", [NH, 64, 1024], BF16)
    d_ccin = nc.dram_tensor("ccin", [16, 128, 128], BF16)
    d_ccout = nc.dram_tensor("ccout", [NCORES * 16, 128, 128], BF16)
    d_out = nc.dram_tensor("outT", [128, NCH * 1024], F32, kind="ExternalOutput").ap()
    d_dbg = None
    if DBG:
        d_dbg = nc.dram_tensor("dbg", [128, 6 * 4096], F32, kind="ExternalOutput").ap()

    sb = lambda name, shape, dt: es.enter_context(nc.sbuf_tensor(name, shape, dt))
    sacrif = sb("sacrif", [128, 2048], F32)
    x = sb("x", [128, NCH, 1024], F32)
    actA = sb("actA", [128, NCH, 1024], BF16)
    ar2 = sb("ar2", [128, 16384], BF16)
    ar3 = sb("ar3", [128, 4096], BF16)
    ar4 = sb("ar4", [128, 4096], BF16)
    wpool = sb("wpool", [128, 8192], BF16)
    fland = sb("fland", [128, 4096], F32)
    f8a = sb("f8a", [128, 1024], F8)
    sq = sb("sq", [128, 1024], F32)
    sq2 = sb("sq2", [128, 1024], F32)
    bsb = sb("bsb", [128, 1024], F32)
    ptb = sb("ptb", [128, 2688], BF16)
    rowsA = sb("rowsA", [1, 2560], F32)
    mrow = rowsA[0:1, 0:512]
    m2row = rowsA[0:1, 512:1024]
    vrow = rowsA[0:1, 1024:1536]
    rrow = rowsA[0:1, 1536:2048]
    nrow = rowsA[0:1, 2048:2560]
    rlrow = rowsA[0:1, 0:512]
    rlrow2 = rowsA[0:1, 512:1024]
    bvrow = rowsA[0:1, 1920:2048]
    cbias = sb("cbias", [128, 4], F32)
    fscr = sb("fscr", [128, 4], F32)
    onesf = sb("onesf", [128, 1], F32)
    ones1f = sb("ones1f", [1, 128], F32)
    onesb = sb("onesb", [128, 1], BF16)
    trib = ptb[:, 2560:2688]
    rtb = sb("rtb", [64, 64], BF16)
    biasA = sb("biasA_sb", [128, 32], F32)
    stripeB = sb("stripeB_sb", [128, 8], F32)
    kvnw = sb("kvnw", [128, 4], F32)
    cosv = sb("cosv", [64, 1], F32)
    sinr = sb("sinr", [64, 1], F32)
    signv = sb("signv", [64, 1], F32)
    bqt = sb("bqt", [128, 16], F32)
    bkt = sb("bkt", [128, 16], F32)
    gsc = sb("gsc", [128, 48], F32)
    usc = sb("usc", [128, 48], F32)
    dsc = sb("dsc", [128, 16], F32)

    pk = [es.enter_context(nc.psum_tensor(f"pb{i}", [128, 512], F32)) for i in range(8)]

    p = PB(nc, es)

    ckvnl = lambda l: ar2[:, 4096 * l: 4096 * (l + 1)]
    ckvnr = ar3[0:64, 0:4096]
    wkvab = wpool[:, 0:8192]
    wkvafold = ar4[:, 0:1024]
    slab = lambda i: wpool[:, 2048 * i: 2048 * (i + 1)]
    land = lambda i: fland[:, 2048 * i: 2048 * (i + 1)]

    p.dma("sync", biasA[:, :], d_biasA, "L")
    p.dma("sync", stripeB[:, :], d_stripeB, "L")
    p.dma("sync", kvnw[:, :], d_kvnw, "L")
    p.dma("sync", cosv[:, :], d_cos, "L")
    p.dma("sync", sinr[:, :], d_sinr, "L")
    p.dma("sync", signv[:, :], d_signv, "L")
    p.dma("sync", bqt[:, :], d_bqt, "L")
    p.dma("sync", bkt[:, :], d_bkt, "L")
    p.dma("sync", gsc[:, :], d_gsc, "L")
    p.dma("sync", usc[:, :], d_usc, "L")
    p.dma("sync", dsc[:, :], d_dsc, "L")
    p.dma("sync", x[:, :, :], d_x0[:, 0: NCH * 4096].rearrange("p (c t) -> p c t", c=NCH)[:, :, 0:1024], "L")
    p.dma("sync", land(0)[:, :], d_wkvat[:, 0:2048], "L")
    p.dma("sync", land(1)[:, :], d_wkvat[:, 2048:4096], "L")
    ld_all = p.cnt["L"]

    p.do("vector", lambda e: e.memset(cbias[:, 0:1], LN_EPS), inc="V")
    p.do("vector", lambda e: e.memset(cbias[:, 1:2], RMS_EPS), inc="V")
    p.do("vector", lambda e: e.memset(cbias[:, 2:3], -CB), inc="V")
    p.do("vector", lambda e: e.memset(onesf[:, :], 1.0), inc="V")
    p.do("vector", lambda e: e.memset(ones1f[:, :], 1.0), inc="V")
    p.do("vector", lambda e: e.memset(onesb[:, :], 1.0), inc="V")


    t64a = sq2[0:64, 0:64]
    t64b = sq2[0:64, 64:128]
    t64c = sq2[0:64, 128:192]
    t64d = sq2[0:64, 192:256]
    t64o = sq2[0:64, 256:320]
    p.wait("vector", "L", ld_all)
    p.do("vector", lambda e: e.memset(t64o, 1.0), inc="V")
    p.do("vector", lambda e: e.tensor_scalar(t64a, t64o, cosv[:, :], None, ALU.mult), inc="V")
    p.do("vector", lambda e: e.tensor_scalar(t64b, t64o, sinr[:, :], signv[:, :], ALU.mult, ALU.mult), inc="V")
    p.wait("gpsimd", "V")
    p.do("gpsimd", lambda e: e.affine_select(
        out=t64a, in_=t64a, compare_op=ALU.is_equal, fill=0.0,
        base=0, pattern=[[-1, 64]], channel_multiplier=1), inc="Q")
    p.do("gpsimd", lambda e: e.affine_select(
        out=t64c, in_=t64b, compare_op=ALU.is_equal, fill=0.0,
        base=32, pattern=[[-1, 64]], channel_multiplier=1), inc="Q")
    p.do("gpsimd", lambda e: e.affine_select(
        out=t64d, in_=t64b, compare_op=ALU.is_equal, fill=0.0,
        base=-32, pattern=[[-1, 64]], channel_multiplier=1), inc="Q")
    p.wait("vector", "Q")
    p.do("vector", lambda e: e.tensor_tensor(t64a, t64a, t64c, ALU.add), inc="V")
    p.do("vector", lambda e: e.tensor_tensor(t64a, t64a, t64d, ALU.add), inc="V")
    p.do("vector", lambda e: e.tensor_copy(rtb[:, :], t64a), inc="V")

    p.wait("gpsimd", "L", ld_all)
    p.do("gpsimd", lambda e: e.tensor_copy(wkvab[:, 0:2048], land(0)[:, :]), inc="Q")
    p.do("gpsimd", lambda e: e.tensor_copy(wkvab[:, 2048:4096], land(1)[:, :]), inc="Q")
    v_land01_free = p.cnt["Q"]
    p.wait("sync", "Q", v_land01_free)
    p.dma("sync", land(0)[:, :], d_wkvat[:, 4096:6144], "L")
    p.dma("sync", land(1)[:, :], d_wkvat[:, 6144:8192], "L")
    ld2 = p.cnt["L"]
    p.wait("gpsimd", "L", ld2)
    p.do("gpsimd", lambda e: e.tensor_copy(wkvab[:, 4096:6144], land(0)[:, :]), inc="Q")
    p.do("gpsimd", lambda e: e.tensor_copy(wkvab[:, 6144:8192], land(1)[:, :]), inc="Q")
    p.wait("sync", "Q")
    p.dma("sync", land(0)[0:64, :], d_wkvape, "L")
    ld3 = p.cnt["L"]
    p.wait("gpsimd", "L", ld3)
    p.do("gpsimd", lambda e: e.tensor_copy(ar4[0:64, 1024:3072], land(0)[0:64, :]), inc="Q")
    p.wait("tensor", "Q")
    p.wait("tensor", "V")
    for c in range(NCH):
        if c >= 1:
            p.wait("tensor", "S")
        p.do("tensor", lambda e, c=c: e.matmul(
            pk[4][:, 0:64], ar4[0:64, 1024 + 128 * c:1024 + 128 * (c + 1)], rtb[:, :],
            start=True, stop=True), inc="P")
        p.wait("scalar", "P")
        p.do("scalar", lambda e, c=c: e.copy(wkvafold[:, 64 * c:64 * (c + 1)], pk[4][:, 0:64]), inc="S")

    p.join()

    xa = fland[:, 0:2048].rearrange("p (c t) -> p c t", c=NCH)
    for sbk in range(32):
        t0 = 128 * sbk
        p.wait("sync", "P")
        p.wait("sync", "V")
        p.wait("sync", "S")
        p.dma("sync", xa[:, :, :],
              d_x0.rearrange("p (c t) -> p c t", c=NCH)[:, :, t0:t0 + 128], "L")
        lA = p.cnt["L"]
        p.wait("scalar", "L", lA)
        p.wait("tensor", "L", lA)
        for c in range(NCH):
            sqs = sq[:, 128 * (c % 2): 128 * (c % 2) + 128]
            if c >= 2:
                p.wait("scalar", "P")
            p.do("scalar", lambda e, c=c, s=sqs: e.activation(s, xa[:, c, :], AF.Square), inc="S")
            p.wait("tensor", "S")
            p.do("tensor", lambda e, c=c: e.matmul(
                pk[0][0:1, 0:128], onesf[:, :], xa[:, c, :],
                start=(c == 0), stop=(c == NCH - 1)), inc="P")
            p.do("tensor", lambda e, c=c, s=sqs: e.matmul(
                pk[1][0:1, 0:128], onesf[:, :], s,
                start=(c == 0), stop=(c == NCH - 1)), inc="P")
        p.wait("vector", "P")
        p.do("vector", lambda e: e.tensor_scalar(mrow[:, 0:128], pk[0][0:1, 0:128], 1.0 / D, None, ALU.mult), inc="V")
        p.do("vector", lambda e: e.tensor_scalar(m2row[:, 0:128], pk[1][0:1, 0:128], 1.0 / D, None, ALU.mult), inc="V")
        p.do("vector", lambda e: e.tensor_tensor(vrow[:, 0:128], mrow[:, 0:128], mrow[:, 0:128], ALU.mult), inc="V")
        p.do("vector", lambda e: e.tensor_tensor(vrow[:, 0:128], m2row[:, 0:128], vrow[:, 0:128], ALU.subtract), inc="V")
        p.wait("scalar", "V")
        p.do("scalar", lambda e: e.activation(vrow[:, 0:128], vrow[:, 0:128], AF.Ln, bias=cbias[0:1, 0:1]), inc="S")
        p.do("scalar", lambda e: e.activation(rrow[:, 0:128], vrow[:, 0:128], AF.Exp, scale=-0.5), inc="S")
        p.wait("vector", "S")
        p.do("vector", lambda e: e.scalar_tensor_tensor(
            nrow[:, 0:128], mrow[:, 0:128], -1.0, rrow[:, 0:128], ALU.mult, ALU.mult), inc="V")
        p.wait("tensor", "V")
        p.do("tensor", lambda e: e.matmul(pk[2][:, 0:128], ones1f[:, :], rrow[:, 0:128], start=True, stop=True), inc="P")
        p.do("tensor", lambda e: e.matmul(pk[3][:, 0:128], ones1f[:, :], nrow[:, 0:128], start=True, stop=True), inc="P")
        pe_bc = p.cnt["P"]
        lnt = ptb[:, 0:2048].rearrange("p (c t) -> p c t", c=NCH)
        p.wait("vector", "P", pe_bc)
        for c in range(NCH):
            p.do("vector", lambda e, c=c: e.tensor_tensor(bsb[:, 0:128], xa[:, c, :], pk[2][:, 0:128], ALU.mult), inc="V")
            p.do("vector", lambda e, c=c: e.tensor_tensor(lnt[:, c, :], bsb[:, 0:128], pk[3][:, 0:128], ALU.add), inc="V")
            if sbk < 8:
                p.do("vector", lambda e, c=c, t0=t0: e.tensor_copy(actA[:, c, t0:t0 + 128], lnt[:, c, :]), inc="V")
        lnv = p.cnt["V"]
        lnsrc = lambda c: lnt[:, c, :]
        p.wait("tensor", "V", lnv)
        rms_pe = []
        for l in range(4):
            bank = pk[4 + (l % 2)]
            if l >= 2:
                p.wait("tensor", "S")
            for c in range(NCH):
                p.do("tensor", lambda e, l=l, c=c, b=bank: e.matmul(
                    b[:, 0:128], wkvab[:, 512 * c + 128 * l: 512 * c + 128 * (l + 1)], lnsrc(c),
                    start=(c == 0), stop=(c == NCH - 1)), inc="P")
            p.wait("scalar", "P")
            p.do("scalar", lambda e, l=l, b=bank: e.copy(sq2[:, 128 * ((l % 4)): 128 * (l % 4) + 128], b[:, 0:128]), inc="S")
            p.do("scalar", lambda e, l=l, b=bank: e.activation(sq[:, 128 * (l % 2): 128 * (l % 2) + 128], b[:, 0:128], AF.Square), inc="S")
            p.wait("tensor", "S")
            p.do("tensor", lambda e, l=l: e.matmul(
                pk[6][0:1, 0:128], onesf[:, :], sq[:, 128 * (l % 2): 128 * (l % 2) + 128],
                start=(l == 0), stop=(l == 3)), inc="P")
            rms_pe.append(p.cnt["P"])
        for c in range(NCH):
            p.do("tensor", lambda e, c=c: e.matmul(
                pk[7][0:64, 0:128], wkvafold[:, 64 * c: 64 * (c + 1)], lnsrc(c),
                start=(c == 0), stop=(c == NCH - 1)), inc="P")
        pe_rope = p.cnt["P"]
        p.wait("scalar", "P", pe_rope)
        p.do("scalar", lambda e, t0=t0: e.copy(ckvnr[:, t0:t0 + 128], pk[7][0:64, 0:128]), inc="S")
        p.wait("vector", "P", rms_pe[-1])
        p.do("vector", lambda e: e.tensor_scalar(vrow[:, 0:128], pk[6][0:1, 0:128], 1.0 / LORA, None, ALU.mult), inc="V")
        p.wait("scalar", "V")
        p.do("scalar", lambda e: e.activation(vrow[:, 0:128], vrow[:, 0:128], AF.Ln, bias=cbias[0:1, 1:2]), inc="S")
        p.do("scalar", lambda e: e.activation(rrow[:, 0:128], vrow[:, 0:128], AF.Exp, scale=-0.5), inc="S")
        p.wait("tensor", "S")
        p.do("tensor", lambda e: e.matmul(pk[2][:, 0:128], ones1f[:, :], rrow[:, 0:128], start=True, stop=True), inc="P")
        pe_rb = p.cnt["P"]
        p.wait("vector", "P", pe_rb)
        for l in range(4):
            p.do("vector", lambda e, l=l, t0=t0: e.scalar_tensor_tensor(
                ckvnl(l)[:, t0:t0 + 128], sq2[:, 128 * l: 128 * (l + 1)], kvnw[:, l:l + 1],
                pk[2][:, 0:128], ALU.mult, ALU.mult), inc="V")

    p.join()
    p.do("gpsimd", lambda e: e.memset(sq[:, 0:128], 1.0))
    p.do("gpsimd", lambda e: e.affine_select(
        out=sq[:, 0:128], in_=sq[:, 0:128], compare_op=ALU.is_ge, fill=0.0,
        base=0, pattern=[[1, 128]], channel_multiplier=-1), inc="Q")
    p.wait("vector", "Q")
    p.do("vector", lambda e: e.tensor_copy(trib[:, :], sq[:, 0:128]), inc="V")
    p.join()
    if DBG in (1, 2):
        for l in range(4):
            p.dma("gpsimd", d_dbg[:, 4096 * l: 4096 * (l + 1)],
                  ckvnl(l), "G")
        p.dma("gpsimd", d_dbg[0:64, 4 * 4096: 4 * 4096 + 4096], ckvnr, "G")
    if DBG in (1, 2):
        p.dma("gpsimd", d_dbg[:, 5 * 4096: 5 * 4096 + 1024], sq2[:, :], "G")
        p.dma("gpsimd", d_dbg[0:1, 5 * 4096 + 1024: 5 * 4096 + 1024 + 1024],
              rowsA[0:1, 1024:2048], "G")
    if DBG >= 2:
        for c in range(4):
            p.dma("gpsimd", d_dbg[:, 2 * 4096 + 1024 * c: 2 * 4096 + 1024 * (c + 1)],
                  actA[:, c, :], "G")

    if STAGE >= 2:
        for h in range(NH):
            li = h % 2
            p.wait("sync", "Q")
            p.dma("sync", land(li)[:, :], d_wqtn[h], "L")
            lq = p.cnt["L"]
            p.wait("gpsimd", "L", lq)
            p.do("gpsimd", lambda e, li=li: e.tensor_copy(slab(li)[:, :], land(li)[:, :]), inc="Q")
            qn_cast = p.cnt["Q"]
            p.wait("sync", "Q", qn_cast)
            p.dma("sync", land(li)[0:64, :], d_wqpe[h], "L")
            lqp = p.cnt["L"]
            p.wait("gpsimd", "L", lqp)
            p.do("gpsimd", lambda e, li=li: e.tensor_copy(slab(2 + li)[0:64, :], land(li)[0:64, :]), inc="Q")
            pe_cast = p.cnt["Q"]
            fold = ptb[:, 1024 * li: 1024 * li + 1024]
            p.wait("tensor", "Q", pe_cast)
            for c in range(NCH):
                p.wait("tensor", "S")
                p.do("tensor", lambda e, c=c, li=li: e.matmul(
                    pk[4][:, 0:64], slab(2 + li)[0:64, 128 * c:128 * (c + 1)], rtb[:, :],
                    start=True, stop=True), inc="P")
                p.wait("scalar", "P")
                p.do("scalar", lambda e, c=c, f=fold: e.copy(f[:, 64 * c: 64 * (c + 1)], pk[4][:, 0:64]), inc="S")
            fold_done = p.cnt["S"]
            qn_sb = ar4[:, 1024 * li: 1024 * li + 1024]
            qr_sb = ar4[0:64, 2048 + 1024 * li: 2048 + 1024 * li + 1024]
            p.wait("tensor", "S", fold_done)
            for half in range(2):
                for c in range(NCH):
                    p.do("tensor", lambda e, c=c, half=half, li=li: e.matmul(
                        pk[half][:, :], slab(li)[:, 128 * c:128 * (c + 1)],
                        actA[:, c, 512 * half: 512 * half + 512],
                        start=(c == 0), stop=(c == NCH - 1)), inc="P")
                for c in range(NCH):
                    p.do("tensor", lambda e, c=c, half=half, f=fold: e.matmul(
                        pk[2 + half][0:64, :], f[:, 64 * c: 64 * (c + 1)],
                        actA[:, c, 512 * half: 512 * half + 512],
                        start=(c == 0), stop=(c == NCH - 1)), inc="P")
            qpe_done = p.cnt["P"]
            p.wait("scalar", "P", qpe_done)
            for half in range(2):
                p.do("scalar", lambda e, half=half, q=qn_sb: e.copy(q[:, 512 * half:512 * half + 512], pk[half][:, :]), inc="S")
                p.do("scalar", lambda e, half=half, q=qr_sb: e.copy(q[0:64, 512 * half:512 * half + 512], pk[2 + half][0:64, :]), inc="S")
            qev = p.cnt["S"]
            p.wait("gpsimd", "S", qev)
            p.dma("gpsimd", d_qspn[h], qn_sb, "G")
            p.dma("gpsimd", d_qspr[h], qr_sb[0:64, :], "G")

    p.join()

    if STAGE >= 2:
        ksb = lambda i: ptb[:, 2048 + 128 * i: 2048 + 128 * (i + 1)]
        vsb = lambda i: ptb[:, 2304 + 128 * i: 2304 + 128 * (i + 1)]
        pt = lambda i: ptb[:, 1024 * i: 1024 * (i + 1)]
        lpart = sq2
        for h in range(NH):
            li = h % 2
            qn_sb = ar4[:, 1024 * li: 1024 * li + 1024]
            qr_sb = ar4[0:64, 2048 + 1024 * li: 2048 + 1024 * li + 1024]
            p.wait("gpsimd", "P")
            p.dma("gpsimd", qn_sb, d_qspn[h], "G")
            p.dma("gpsimd", qr_sb[0:64, :], d_qspr[h], "G")
            p.wait("sync", "V")
            p.dma("sync", land(li)[:, 0:512], d_wkbk[h], "L")
            p.dma("sync", land(li)[:, 512:1024], d_wkbv[h], "L")
            lw = p.cnt["L"]
            p.wait("gpsimd", "L", lw)
            wkh = wpool[:, 2048 * (li): 2048 * li + 512]
            wvh = wpool[:, 2048 * li + 512: 2048 * li + 1024]
            if DBG == 3 and h == 0:
                p.dma("gpsimd", d_dbg[:, 22528:23552], land(li)[:, 0:1024], "G")
            p.wait("vector", "L", lw)
            p.wait("vector", "P")
            p.do("vector", lambda e, w=wkh, li=li: e.tensor_copy(w, land(li)[:, 0:512]), inc="V")
            p.do("vector", lambda e, w=wvh, li=li: e.tensor_copy(w, land(li)[:, 512:1024]), inc="V")
            wcast = p.cnt["V"]
            qload = p.cnt["G"]
            p.wait("vector", "P")
            p.do("vector", lambda e: e.memset(lpart[:, :], 0.0), inc="V")
            if DBG == 3 and h == 0:
                p.wait("gpsimd", "V", wcast)
                p.dma("gpsimd", d_dbg[:, 21504:22016], wkh, "G")
                p.dma("gpsimd", d_dbg[:, 22016:22528], wvh, "G")
                gdump = p.cnt["G"]
                p.wait("tensor", "G", gdump)
                p.wait("sync", "G", gdump)
                p.wait("tensor", "G")
            p.wait("tensor", "G", qload)
            p.wait("tensor", "V", wcast)

            sc_evac = {}
            dv_evac = {}
            sc_exp = {}
            dv_mask = {}
            pe_S = {}
            pe_AV = {}

            def emit_kv(t, wkh=wkh, wvh=wvh):
                if t >= 1:
                    p.wait("tensor", "S", sc_evac.get(t - 1, 0))
                    p.wait("tensor", "V", dv_evac.get(t - 1, 0))
                for l in range(4):
                    p.do("tensor", lambda e, l=l, t=t, w=wkh: e.matmul(
                        pk[6][:, 0:128], w[:, 128 * l: 128 * (l + 1)],
                        ckvnl(l)[:, 128 * t: 128 * (t + 1)],
                        start=(l == 0), stop=(l == 3)), inc="P")
                for l in range(4):
                    p.do("tensor", lambda e, l=l, t=t, w=wvh: e.matmul(
                        pk[7][:, 0:128], ckvnl(l)[:, 128 * t: 128 * (t + 1)],
                        w[:, 128 * l: 128 * (l + 1)],
                        start=(l == 0), stop=(l == 3)), inc="P")
                kvp = p.cnt["P"]
                p.wait("scalar", "P", kvp)
                p.do("scalar", lambda e, t=t: e.copy(ksb(t % 2), pk[6][:, 0:128]), inc="S")
                p.wait("vector", "P", kvp)
                p.do("vector", lambda e, t=t: e.tensor_copy(vsb(t % 2), pk[7][:, 0:128]), inc="V")
                sc_evac[t] = p.cnt["S"]
                dv_evac[t] = p.cnt["V"]

            def emit_S(t, qn_sb=qn_sb, qr_sb=qr_sb):
                p.wait("tensor", "S", sc_evac[t])
                p.wait("tensor", "V", dv_evac[t])
                if t >= 2:
                    p.wait("tensor", "S", sc_exp[t - 2])
                for half in range(2):
                    bank = pk[2 + (t % 2) * 2 + half]
                    p.do("tensor", lambda e, t=t, half=half, b=bank, q=qn_sb: e.matmul(
                        b[:, :], ksb(t % 2), q[:, 512 * half: 512 * half + 512],
                        start=True, stop=False), inc="P")
                    p.do("tensor", lambda e, t=t, half=half, b=bank, q=qr_sb: e.matmul(
                        b[:, :], ckvnr[:, 128 * t: 128 * (t + 1)],
                        q[0:64, 512 * half: 512 * half + 512],
                        start=False, stop=True), inc="P")
                pe_S[t] = p.cnt["P"]
                p.wait("scalar", "P", pe_S[t])
                if t >= 2:
                    p.wait("scalar", "V", dv_mask[t - 1])
                    p.wait("scalar", "P", pe_AV.get(t - 2, 0))
                for half in range(2):
                    bank = pk[2 + (t % 2) * 2 + half]
                    p.do("scalar", lambda e, t=t, half=half, b=bank: e.activation(
                        pt(t % 2)[:, 512 * half: 512 * half + 512], b[:, :], AF.Exp,
                        bias=biasA[:, t:t + 1], scale=SCALE_A), inc="S")
                sc_exp[t] = p.cnt["S"]
                p.wait("vector", "S", sc_exp[t])
                if t < 8:
                    if t > 0:
                        p.do("vector", lambda e, t=t: e.memset(pt(t % 2)[:, 0:128 * t], 0.0), inc="V")
                    p.do("vector", lambda e, t=t: e.tensor_tensor(
                        pt(t % 2)[:, 128 * t: 128 * (t + 1)],
                        pt(t % 2)[:, 128 * t: 128 * (t + 1)], trib[:, :], ALU.mult), inc="V")
                if t % 2 == 1:
                    p.do("vector", lambda e: e.tensor_tensor(
                        bsb[:, :], pt(0), pt(1), ALU.add), inc="V")
                    p.do("vector", lambda e: e.tensor_tensor(lpart[:, :], lpart[:, :], bsb[:, :], ALU.add), inc="V")
                dv_mask[t] = p.cnt["V"]

            def emit_AV(t):
                p.wait("tensor", "S", sc_exp[t])
                p.wait("tensor", "V", dv_mask[t])
                for half in range(2):
                    p.do("tensor", lambda e, t=t, half=half: e.matmul(
                        pk[half][:, :], vsb(t % 2), pt(t % 2)[:, 512 * half: 512 * half + 512],
                        start=(t == 0), stop=(t == 31)), inc="P")
                pe_AV[t] = p.cnt["P"]

            emit_kv(0)
            if DBG == 5 and h in (0, 1):
                p.wait("gpsimd", "S", sc_evac[0])
                p.wait("gpsimd", "V", dv_evac[0])
                base = 19456 + h * 1024
                p.dma("gpsimd", d_dbg[:, base: base + 128], ksb(0), "G")
                p.dma("gpsimd", d_dbg[:, base + 128: base + 256], vsb(0), "G")
                p.wait("tensor", "G")
            for t in range(32):
                emit_S(t)
                if DBG == 5 and h in (0, 1) and t == 0:
                    p.wait("gpsimd", "V", dv_mask[0])
                    base = 19456 + h * 1024
                    p.dma("gpsimd", d_dbg[:, base + 256: base + 1024], pt(0)[:, 0:768], "G")
                    p.wait("tensor", "G")
                if t < 31:
                    emit_kv(t + 1)
                emit_AV(t)
            p.wait("tensor", "V")
            p.wait("tensor", "S")
            p.do("tensor", lambda e: e.matmul(pk[2][0:1, :], onesf[:, :], lpart[:, 0:512], start=True, stop=True), inc="P")
            p.do("tensor", lambda e: e.matmul(pk[4][0:1, :], onesf[:, :], lpart[:, 512:1024], start=True, stop=True), inc="P")
            lsum = p.cnt["P"]
            p.wait("vector", "P", lsum)
            p.do("vector", lambda e: e.reciprocal(rlrow, pk[2][0:1, :]), inc="V")
            p.do("vector", lambda e: e.reciprocal(rlrow2, pk[4][0:1, :]), inc="V")
            rec = p.cnt["V"]
            p.wait("tensor", "V", rec)
            p.do("tensor", lambda e: e.matmul(pk[3][:, :], ones1f[:, :], rlrow, start=True, stop=True), inc="P")
            p.do("tensor", lambda e: e.matmul(pk[5][:, :], ones1f[:, :], rlrow2, start=True, stop=True), inc="P")
            bc = p.cnt["P"]
            p.wait("scalar", "P", bc)
            p.do("scalar", lambda e: e.copy(bsb[:, 0:512], pk[3][:, :]))
            p.do("scalar", lambda e: e.copy(bsb[:, 512:1024], pk[5][:, :]))
            p.do("scalar", lambda e: e.copy(fscr[:, 0:1], cbias[:, 3:4]), inc="S")
            bcs = p.cnt["S"]
            if False and h == 0:
                p.do("scalar", lambda e: e.copy(sq[:, 0:512], pk[0][:, :]), inc="S")
                p.do("scalar", lambda e: e.copy(sq[:, 512:1024], pk[1][:, :]), inc="S")
                dbgs = p.cnt["S"]
                p.wait("gpsimd", "S", dbgs)
                p.wait("gpsimd", "V")
                p.dma("gpsimd", d_dbg[:, 19456:20480], sq[:, :], "G")
                p.dma("gpsimd", d_dbg[:, 20480:21504], sq2[:, :], "G")
                p.dma("gpsimd", d_dbg[0:1, 21504:21504 + 1152], rowsA[0:1, 2560:3712], "G")
                p.wait("scalar", "G")
                p.wait("tensor", "S", dbgs)
                p.wait("tensor", "G")
            p.wait("vector", "S", bcs)
            for half in range(2):
                p.do("vector", lambda e, h=h, half=half: e.tensor_tensor(
                    actA[:, h, 512 * half: 512 * half + 512], pk[half][:, :],
                    bsb[:, 512 * half: 512 * half + 512], ALU.mult), inc="V")
            p.wait("tensor", "V")

    p.join()
    if DBG == 3:
        for c in range(NCH):
            p.dma("gpsimd", d_dbg[:, 1024 * c: 1024 * (c + 1)], actA[:, c, :], "G")
        p.dma("gpsimd", d_dbg[:, 16384:17408], d_qspn[0], "G")
        p.dma("gpsimd", d_dbg[0:64, 17408:18432], d_qspr[0], "G")
        p.dma("gpsimd", d_dbg[:, 18432:19456], sq2[:, :], "G")
        p.join()

    if STAGE >= 2:
        cast_done = {}
        for fo in range(NCH):
            li = fo % 2
            if fo >= 2:
                p.wait("sync", "Q", cast_done[fo - 2])
            p.dma("sync", land(li)[:, :], d_wot[fo], "L")
            lw = p.cnt["L"]
            p.wait("gpsimd", "L", lw)
            p.wait("gpsimd", "P")
            p.do("gpsimd", lambda e, li=li: e.tensor_copy(slab(li)[:, :], land(li)[:, :]), inc="Q")
            cast_done[fo] = p.cnt["Q"]
            p.wait("tensor", "Q", cast_done[fo])
            if fo >= 2:
                p.wait("tensor", "V")
            for half in range(2):
                bank = pk[(fo % 2) * 2 + half]
                for ch in range(NCH):
                    p.do("tensor", lambda e, ch=ch, half=half, li=li, b=bank: e.matmul(
                        b[:, :], slab(li)[:, 128 * ch: 128 * (ch + 1)],
                        actA[:, ch, 512 * half: 512 * half + 512],
                        start=(ch == 0), stop=(ch == NCH - 1)), inc="P")
            wop = p.cnt["P"]
            p.wait("vector", "P", wop)
            for half in range(2):
                bank = pk[(fo % 2) * 2 + half]
                p.do("vector", lambda e, fo=fo, half=half, b=bank: e.tensor_tensor(
                    x[:, fo, 512 * half: 512 * half + 512],
                    x[:, fo, 512 * half: 512 * half + 512], b[:, :], ALU.add), inc="V")

    p.join()

    def emit_ln():
        for c in range(NCH):
            sqs = sq if c % 2 == 0 else bsb
            if c >= 2:
                p.wait("scalar", "P")
            p.do("scalar", lambda e, c=c, s=sqs: e.activation(s[:, :], x[:, c, :], AF.Square), inc="S")
            p.wait("tensor", "S")
            for half in range(2):
                p.do("tensor", lambda e, c=c, half=half: e.matmul(
                    pk[half][0:1, :], onesf[:, :], x[:, c, 512 * half: 512 * half + 512],
                    start=(c == 0), stop=(c == NCH - 1)), inc="P")
                p.do("tensor", lambda e, c=c, half=half, s=sqs: e.matmul(
                    pk[2 + half][0:1, :], onesf[:, :], s[:, 512 * half: 512 * half + 512],
                    start=(c == 0), stop=(c == NCH - 1)), inc="P")
        sums = p.cnt["P"]
        p.wait("vector", "P", sums)
        for half in range(2):
            if half == 1:
                p.wait("vector", "P")
            p.do("vector", lambda e, half=half: e.tensor_scalar(
                mrow, pk[half][0:1, :], 1.0 / D, None, ALU.mult), inc="V")
            p.do("vector", lambda e, half=half: e.tensor_scalar(
                m2row, pk[2 + half][0:1, :], 1.0 / D, None, ALU.mult), inc="V")
            p.do("vector", lambda e: e.tensor_tensor(vrow, mrow, mrow, ALU.mult), inc="V")
            p.do("vector", lambda e: e.tensor_tensor(vrow, m2row, vrow, ALU.subtract), inc="V")
            st = p.cnt["V"]
            p.wait("scalar", "V", st)
            p.do("scalar", lambda e: e.activation(vrow, vrow, AF.Ln, bias=cbias[0:1, 0:1]), inc="S")
            p.do("scalar", lambda e: e.activation(rrow, vrow, AF.Exp, scale=-0.5), inc="S")
            p.do("scalar", lambda e: e.copy(fscr[:, 0:1], cbias[:, 3:4]), inc="S")
            st2 = p.cnt["S"]
            p.wait("vector", "S", st2)
            p.do("vector", lambda e: e.scalar_tensor_tensor(
                nrow, mrow, -1.0, rrow, ALU.mult, ALU.mult), inc="V")
            st3 = p.cnt["V"]
            p.wait("tensor", "V", st3)
            p.do("tensor", lambda e, half=half: e.matmul(
                pk[4 + half][:, :], ones1f[:, :], rrow, start=True, stop=True), inc="P")
            p.do("tensor", lambda e, half=half: e.matmul(
                pk[6 + half][:, :], ones1f[:, :], nrow, start=True, stop=True), inc="P")
        bc = p.cnt["P"]
        p.wait("vector", "P", bc)
        for c in range(NCH):
            for half in range(2):
                hs = slice(512 * half, 512 * half + 512)
                p.do("vector", lambda e, c=c, half=half, hs=hs: e.tensor_tensor(
                    sq[:, 0:512], x[:, c, hs], pk[4 + half][:, :], ALU.mult), inc="V")
                p.do("vector", lambda e, c=c, half=half, hs=hs: e.tensor_tensor(
                    actA[:, c, hs], sq[:, 0:512], pk[6 + half][:, :], ALU.add), inc="V")

    def emit_ffn():
        fuse = lambda j: ar2[:, 1024 * j: 1024 * (j + 1)]
        dslab = lambda j: ar2[:, 4096 + 2048 * j: 4096 + 2048 * (j + 1)]
        def gsl(j):
            return ar2[:, 12288:14336] if j % 2 == 0 else ar4[:, 0:2048]

        def usl(j):
            return ar3[:, 0:2048] if j % 2 == 0 else ar4[:, 2048:4096]

        prev_consumed = {"g": {}, "u": {}, "d": {}}
        for sg in range(12):
            for j in range(4):
                g = 4 * sg + j
                li = g % 2
                p.wait("sync", "Q")
                p.dma("sync", land(li)[:, :], d_gw[g], "L")
                lg = p.cnt["L"]
                p.wait("gpsimd", "L", lg)
                p.wait("gpsimd", "P", prev_consumed["g"].get(j % 2, 0))
                p.do("gpsimd", lambda e, li=li: e.tensor_copy(f8a[:, :], land(li)[:, 0:1024]), inc="Q")
                p.do("gpsimd", lambda e, j=j: e.tensor_copy(gsl(j % 2)[:, 0:1024], f8a[:, :]), inc="Q")
                p.do("gpsimd", lambda e, li=li: e.tensor_copy(f8a[:, :], land(li)[:, 1024:2048]), inc="Q")
                p.do("gpsimd", lambda e, j=j: e.tensor_copy(gsl(j % 2)[:, 1024:2048], f8a[:, :]), inc="Q")
                gcast = p.cnt["Q"]
                p.wait("sync", "Q")
                p.dma("sync", land(1 - li)[:, :], d_uw[g], "L")
                lu = p.cnt["L"]
                p.wait("gpsimd", "L", lu)
                p.wait("gpsimd", "P", prev_consumed["u"].get(j % 2, 0))
                p.do("gpsimd", lambda e, li=li: e.tensor_copy(f8a[:, :], land(1 - li)[:, 0:1024]), inc="Q")
                p.do("gpsimd", lambda e, j=j: e.tensor_copy(usl(j % 2)[:, 0:1024], f8a[:, :]), inc="Q")
                p.do("gpsimd", lambda e, li=li: e.tensor_copy(f8a[:, :], land(1 - li)[:, 1024:2048]), inc="Q")
                p.do("gpsimd", lambda e, j=j: e.tensor_copy(usl(j % 2)[:, 1024:2048], f8a[:, :]), inc="Q")
                ucast = p.cnt["Q"]
                p.wait("sync", "Q")
                p.dma("sync", land(li)[:, :], d_dw[g], "L")
                ldw = p.cnt["L"]
                p.wait("gpsimd", "L", ldw)
                p.wait("gpsimd", "P", prev_consumed["d"].get(j, 0))
                p.do("gpsimd", lambda e, li=li: e.tensor_copy(f8a[:, :], land(li)[:, 0:1024]), inc="Q")
                p.do("gpsimd", lambda e, j=j: e.tensor_copy(dslab(j)[:, 0:1024], f8a[:, :]), inc="Q")
                p.do("gpsimd", lambda e, li=li: e.tensor_copy(f8a[:, :], land(li)[:, 1024:2048]), inc="Q")
                p.do("gpsimd", lambda e, j=j: e.tensor_copy(dslab(j)[:, 1024:2048], f8a[:, :]), inc="Q")
                dcast = p.cnt["Q"]
                p.wait("tensor", "Q", gcast)
                p.wait("tensor", "S")
                p.wait("tensor", "V")
                for half in range(2):
                    for c in range(NCH):
                        p.do("tensor", lambda e, c=c, half=half, j=j: e.matmul(
                            pk[half][:, :], gsl(j % 2)[:, 128 * c: 128 * (c + 1)],
                            actA[:, c, 512 * half: 512 * half + 512],
                            start=(c == 0), stop=(c == NCH - 1)), inc="P")
                prev_consumed["g"][j % 2] = p.cnt["P"]
                gmm = p.cnt["P"]
                p.wait("scalar", "P", gmm)
                p.wait("scalar", "V")
                for half in range(2):
                    p.do("scalar", lambda e, half=half, g=g: e.activation(
                        sq[:, 512 * half: 512 * half + 512], pk[half][:, :], AF.Silu,
                        scale=gsc[:, g:g + 1]), inc="S")
                p.do("scalar", lambda e: e.copy(fscr[:, 0:1], cbias[:, 3:4]), inc="S")
                silu_done = p.cnt["S"]
                p.wait("tensor", "Q", ucast)
                p.wait("tensor", "V")
                for half in range(2):
                    for c in range(NCH):
                        p.do("tensor", lambda e, c=c, half=half, j=j: e.matmul(
                            pk[2 + half][:, :], usl(j % 2)[:, 128 * c: 128 * (c + 1)],
                            actA[:, c, 512 * half: 512 * half + 512],
                            start=(c == 0), stop=(c == NCH - 1)), inc="P")
                prev_consumed["u"][j % 2] = p.cnt["P"]
                umm = p.cnt["P"]
                p.wait("vector", "P", umm)
                for half in range(2):
                    p.do("vector", lambda e, half=half, g=g: e.tensor_scalar(
                        sq2[:, 512 * half: 512 * half + 512], pk[2 + half][:, :],
                        usc[:, g:g + 1], None, ALU.mult), inc="V")
                p.wait("vector", "S", silu_done)
                p.do("vector", lambda e, j=j: e.tensor_tensor(fuse(j), sq[:, :], sq2[:, :], ALU.mult), inc="V")
                if DBG == 7 and g == 0:
                    fv = p.cnt["V"]
                    p.wait("gpsimd", "V", fv)
                    p.dma("gpsimd", d_dbg[:, 0:1024], sq[:, :], "G")
                    p.dma("gpsimd", d_dbg[:, 1024:2048], sq2[:, :], "G")
                    p.dma("gpsimd", d_dbg[:, 2048:3072], fuse(0), "G")
                    p.dma("gpsimd", d_dbg[:, 3072:5120], gsl(0), "G")
                    p.dma("gpsimd", d_dbg[:, 5120:7168], usl(0), "G")
                    gdmp = p.cnt["G"]
                    p.wait("vector", "G", gdmp)
                    p.wait("scalar", "G", gdmp)
                    p.wait("sync", "G", gdmp)
            fuse_done = p.cnt["V"]
            dcast_all = p.cnt["Q"]
            p.wait("tensor", "V", fuse_done)
            p.wait("tensor", "Q", dcast_all)
            dstt = {}
            for fo in range(NCH):
                pair = 4 + (fo % 2) * 2
                p.wait("tensor", "V", dstt.get(fo - 2, 0))
                for half in range(2):
                    bank = pk[pair + half]
                    for j in range(4):
                        p.do("tensor", lambda e, fo=fo, half=half, j=j, b=bank: e.matmul(
                            b[:, :], dslab(j)[:, 128 * fo: 128 * (fo + 1)],
                            fuse(j)[:, 512 * half: 512 * half + 512],
                            start=(j == 0), stop=(j == 3)), inc="P")
                dmm = p.cnt["P"]
                p.wait("vector", "P", dmm)
                for half in range(2):
                    bank = pk[pair + half]
                    p.do("vector", lambda e, fo=fo, half=half, b=bank: e.scalar_tensor_tensor(
                        x[:, fo, 512 * half: 512 * half + 512], b[:, :], dsc[:, fo:fo + 1],
                        x[:, fo, 512 * half: 512 * half + 512], ALU.mult, ALU.add), inc="V")
                dstt[fo] = p.cnt["V"]
                if DBG == 8 and sg == 0 and fo == 0:
                    p.wait("gpsimd", "V", dstt[fo])
                    p.dma("gpsimd", d_dbg[:, 0:1024], x[:, 0, :], "G")
                    for jj in range(4):
                        p.dma("gpsimd", d_dbg[:, 1024 + 1024 * jj: 2048 + 1024 * jj],
                              fuse(jj), "G")
                    p.dma("gpsimd", d_dbg[:, 5120:7168], dslab(0), "G")
                    gdmp = p.cnt["G"]
                    p.wait("tensor", "G", gdmp)
                    p.wait("vector", "G", gdmp)
            for j in range(4):
                prev_consumed["d"][j] = p.cnt["P"]

    if STAGE >= 3:
        emit_ln()
        p.join()
        if DBG == 6:
            for c in range(NCH):
                p.dma("gpsimd", d_dbg[:, 1024 * c: 1024 * (c + 1)], actA[:, c, :], "G")
            p.join()
        emit_ffn()
        p.join()

    if STAGE >= 4:
        emit_ln()
        p.join()
        qb_ = lambda i: ar2[:, 1024 * i: 1024 * (i + 1)]
        kb_ = lambda i: ar2[:, 2048 + 1024 * i: 2048 + 1024 * (i + 1)]
        vb_ = lambda i: ar2[:, 4096 + 1024 * i: 4096 + 1024 * (i + 1)]
        kpre = ar2[:, 6144:7168]
        vpre = ar2[:, 7168:8192]
        stash = lambda hh, lo, hi: ar2[:, 8192 + 384 * hh + lo: 8192 + 384 * hh + hi]

        def qkv_head(h):
            li = h % 2
            p.wait("sync", "Q")
            p.dma("sync", land(li)[:, :], d_wqb[h], "L")
            l1 = p.cnt["L"]
            p.wait("gpsimd", "L", l1)
            p.wait("gpsimd", "P")
            p.do("gpsimd", lambda e, li=li: e.tensor_copy(slab(li)[:, :], land(li)[:, :]), inc="Q")
            c1 = p.cnt["Q"]
            p.wait("tensor", "Q", c1)
            p.wait("tensor", "S")
            p.wait("tensor", "V")
            for half in range(2):
                for c in range(NCH):
                    p.do("tensor", lambda e, c=c, half=half, li=li: e.matmul(
                        pk[half][:, :], slab(li)[:, 128 * c: 128 * (c + 1)],
                        actA[:, c, 512 * half: 512 * half + 512],
                        start=(c == 0), stop=(c == NCH - 1)), inc="P")
            qmm = p.cnt["P"]
            p.wait("scalar", "P", qmm)
            for half in range(2):
                p.do("scalar", lambda e, half=half, h=h, li=li: e.activation(
                    qb_(li)[:, 512 * half: 512 * half + 512], pk[half][:, :], AF.Identity,
                    bias=bqt[:, h:h + 1]), inc="S")
            p.wait("sync", "Q")
            p.dma("sync", land(1 - li)[:, :], d_wkb[h], "L")
            l2 = p.cnt["L"]
            p.wait("gpsimd", "L", l2)
            p.do("gpsimd", lambda e, li=li: e.tensor_copy(slab(2 + li)[:, :], land(1 - li)[:, :]), inc="Q")
            c2 = p.cnt["Q"]
            p.wait("tensor", "Q", c2)
            for half in range(2):
                for c in range(NCH):
                    p.do("tensor", lambda e, c=c, half=half, li=li: e.matmul(
                        pk[2 + half][:, :], slab(2 + li)[:, 128 * c: 128 * (c + 1)],
                        actA[:, c, 512 * half: 512 * half + 512],
                        start=(c == 0), stop=(c == NCH - 1)), inc="P")
            kmm = p.cnt["P"]
            p.wait("scalar", "P", kmm)
            for half in range(2):
                p.do("scalar", lambda e, half=half, h=h, li=li: e.activation(
                    kb_(li)[:, 512 * half: 512 * half + 512], pk[2 + half][:, :], AF.Identity,
                    bias=bkt[:, h:h + 1]), inc="S")
            p.wait("sync", "P")
            p.dma("sync", bvrow, d_bvr[0:1, 128 * h: 128 * (h + 1)], "L")
            p.wait("sync", "Q")
            p.dma("sync", land(li)[:, :], d_wvb[h], "L")
            l3 = p.cnt["L"]
            p.wait("gpsimd", "L", l3)
            vsl = ar2[:, 11264 + 2048 * li: 11264 + 2048 * (li + 1)]
            p.do("gpsimd", lambda e, v=vsl, li=li: e.tensor_copy(v, land(li)[:, :]), inc="Q")
            c3 = p.cnt["Q"]
            p.wait("tensor", "Q", c3)
            for tc in range(8):
                if tc >= 1:
                    p.wait("tensor", "S")
                for c in range(NCH):
                    p.do("tensor", lambda e, c=c, tc=tc, v=vsl: e.matmul(
                        pk[4][:, 0:128], actA[:, c, 128 * tc: 128 * (tc + 1)],
                        v[:, 128 * c: 128 * (c + 1)],
                        start=(c == 0), stop=False), inc="P")
                p.do("tensor", lambda e, tc=tc: e.matmul(
                    pk[4][:, 0:128], ones1f[:, :], bvrow,
                    start=False, stop=True), inc="P")
                vmm = p.cnt["P"]
                p.wait("scalar", "P", vmm)
                p.do("scalar", lambda e, tc=tc, li=li: e.copy(
                    vb_(li)[:, 128 * tc: 128 * (tc + 1)], pk[4][:, 0:128]), inc="S")

        def group_attn(h, li, q0, k0chunks, v0chunks, xcols):
            Q = q0.shape[-1]
            nch_ = len(k0chunks)
            ptoff = 0
            ptslices = []
            for idx, (kap, bias_ap, tri_at, qsub) in enumerate(k0chunks):
                qs = q0 if qsub is None else q0[:, qsub]
                Qc = qs.shape[-1]
                bank = pk[5] if idx % 2 == 0 else pk[6]
                p.wait("tensor", "S")
                p.do("tensor", lambda e, kap=kap, qs=qs, b=bank, Qc=Qc: e.matmul(
                    b[:, 0:Qc], kap, qs, start=True, stop=True), inc="P")
                smm = p.cnt["P"]
                pts = ptb[:, ptoff: ptoff + Qc]
                ptslices.append((pts, qsub, Qc))
                ptoff += Qc
                p.wait("scalar", "P", smm)
                if bias_ap is None:
                    p.do("scalar", lambda e, b=bank, pts=pts, Qc=Qc: e.activation(
                        pts, b[:, 0:Qc], AF.Exp, bias=cbias[:, 2:3], scale=SCALE_B), inc="S")
                else:
                    p.do("scalar", lambda e, b=bank, pts=pts, Qc=Qc, ba=bias_ap: e.activation(
                        pts, b[:, 0:Qc], AF.Exp, bias=ba, scale=SCALE_B), inc="S")
                expd = p.cnt["S"]
                if tri_at is not None:
                    p.wait("vector", "S", expd)
                    p.do("vector", lambda e, pts=pts, ta=tri_at: e.tensor_tensor(
                        pts[:, ta:ta + 128], pts[:, ta:ta + 128], trib[:, :], ALU.mult), inc="V")
            maskd = p.cnt["V"]
            expall = p.cnt["S"]
            p.wait("tensor", "S", expall)
            p.wait("tensor", "V", maskd)
            for idx, (pts, qsub, Qc) in enumerate(ptslices):
                dst = pk[7][0:1, 0:Q] if qsub is None else pk[7][0:1, qsub]
                p.do("tensor", lambda e, pts=pts, d=dst, st=(idx == 0), sp=(idx == nch_ - 1): e.matmul(
                    d, onesb[:, :], pts, start=st, stop=sp), inc="P")
            for idx, ((pts, qsub, Qc), vap) in enumerate(zip(ptslices, v0chunks)):
                dst = pk[0][:, 0:Q] if qsub is None else pk[0][:, qsub]
                p.do("tensor", lambda e, pts=pts, d=dst, v=vap, st=(idx == 0), sp=(idx == nch_ - 1): e.matmul(
                    d, v, pts, start=st, stop=sp), inc="P")
            lav = p.cnt["P"]
            p.wait("vector", "P", lav)
            p.do("vector", lambda e, Q=Q: e.reciprocal(rlrow[:, 0:Q], pk[7][0:1, 0:Q]), inc="V")
            rc = p.cnt["V"]
            p.wait("tensor", "V", rc)
            p.wait("tensor", "S")
            p.do("tensor", lambda e, Q=Q: e.matmul(pk[4][:, 0:Q], ones1f[:, :], rlrow[:, 0:Q],
                                                   start=True, stop=True), inc="P")
            bc = p.cnt["P"]
            p.wait("scalar", "P", bc)
            p.do("scalar", lambda e, Q=Q: e.copy(bsb[:, 0:Q], pk[4][:, 0:Q]))
            p.do("scalar", lambda e: e.copy(fscr[:, 0:1], cbias[:, 3:4]), inc="S")
            bcs = p.cnt["S"]
            if DBG in (9, 11) and getattr(group_attn, "_dump", False):
                group_attn._dump = False
                p.do("scalar", lambda e, Q=Q: e.copy(sq2[:, 0:Q], pk[0][:, 0:Q]), inc="S")
                p.do("scalar", lambda e, Q=Q: e.copy(sq2[0:1, 512:512 + Q], pk[7][0:1, 0:Q]), inc="S")
                d9 = p.cnt["S"]
                p.wait("gpsimd", "S", d9)
                p.dma("gpsimd", d_dbg[:, 3072:4096], ptb[:, 0:1024], "G")
                p.dma("gpsimd", d_dbg[:, 4096:5120], sq2[:, 0:1024], "G")
                p.dma("gpsimd", d_dbg[0:1, 5120:5632], rlrow[:, :], "G")
                p.dma("gpsimd", d_dbg[:, 5632:6656], bsb[:, :], "G")
                gd9b = p.cnt["G"]
                p.wait("tensor", "G", gd9b)
                p.wait("vector", "G", gd9b)
                p.wait("scalar", "G", gd9b)
            p.wait("vector", "S", bcs)
            p.do("vector", lambda e, Q=Q: e.tensor_tensor(sq[:, 0:Q], pk[0][:, 0:Q], bsb[:, 0:Q], ALU.mult), inc="V")
            p.do("vector", lambda e: e.memset(fscr[:, 1:2], 0.0), inc="V")
            p.do("vector", lambda e, h=h, xc=xcols, Q=Q: e.tensor_tensor(
                x[:, h, xc], x[:, h, xc], sq[:, 0:Q], ALU.add), inc="V")
            p.wait("tensor", "V")

        for h in range(8, 16):
            li = h % 2
            qkv_head(h)
            qkvs = p.cnt["S"]
            if DBG == 9 and h == 8:
                p.wait("gpsimd", "S", qkvs)
                p.dma("gpsimd", d_dbg[:, 0:1024], qb_(li), "G")
                p.dma("gpsimd", d_dbg[:, 1024:2048], kb_(li), "G")
                p.dma("gpsimd", d_dbg[:, 2048:3072], vb_(li), "G")
                gd9 = p.cnt["G"]
                p.wait("tensor", "G", gd9)
                p.wait("vector", "G", gd9)
            p.wait("vector", "S", qkvs)
            hh = h - 8
            p.do("vector", lambda e, s=stash(hh, 0, 128), li=li: e.tensor_copy(s, qb_(li)[:, 0:128]), inc="V")
            p.do("vector", lambda e, s=stash(hh, 128, 256), li=li: e.tensor_copy(s, kb_(li)[:, 0:128]), inc="V")
            p.do("vector", lambda e, s=stash(hh, 256, 384), li=li: e.tensor_copy(s, vb_(li)[:, 0:128]), inc="V")
            p.wait("gpsimd", "S", qkvs)
            p.dma("gpsimd", d_ccin[2 * hh], kb_(li)[:, 896:1024], "G")
            p.dma("gpsimd", d_ccin[2 * hh + 1], vb_(li)[:, 896:1024], "G")
            if DBG == 9 and h == 8:
                group_attn._dump = True
            for g in range(3):
                qc = slice(128 + 256 * g, 128 + 256 * g + 256)
                ks = [(kb_(li)[:, 128 + 256 * g + 0:128 + 256 * g + 128], None, 0, None),
                      (kb_(li)[:, 128 + 256 * g + 128:128 + 256 * g + 256], None, 0, slice(128, 256))]
                vs = [vb_(li)[:, 128 * (2 * g + 1): 128 * (2 * g + 2)],
                      vb_(li)[:, 128 * (2 * g + 2): 128 * (2 * g + 3)]]
                group_attn(h, li, qb_(li)[:, qc], ks, vs, qc)
            qc = slice(896, 1024)
            ks = [(kb_(li)[:, 896:1024], None, 0, None)]
            vs = [vb_(li)[:, 896:1024]]
            group_attn(h, li, qb_(li)[:, qc], ks, vs, qc)

        p.wait("gpsimd", "G")
        p.do("gpsimd", lambda e: e.collective_compute(
            "AllGather", ALU.bypass,
            replica_groups=[list(range(NCORES))],
            ins=[d_ccin.ap()],
            outs=[d_ccout.ap()]), inc="C")

        for h in range(8):
            li = h % 2
            qkv_head(h)
            if DBG == 11 and h == 0:
                group_attn._dump = True
            if DBG == 10 and h == 0:
                d10 = p.cnt["S"]
                p.wait("gpsimd", "S", d10)
                p.dma("gpsimd", d_dbg[:, 0:1024], qb_(li), "G")
                p.dma("gpsimd", d_dbg[:, 1024:2048], kb_(li), "G")
                p.dma("gpsimd", d_dbg[:, 2048:3072], vb_(li), "G")
                gd10 = p.cnt["G"]
                p.wait("tensor", "G", gd10)
                p.wait("vector", "G", gd10)
            for g in range(4):
                qc = slice(256 * g, 256 * g + 256)
                ks = [(kb_(li)[:, 256 * g: 256 * g + 128], None, 0, None),
                      (kb_(li)[:, 256 * g + 128: 256 * g + 256], None, 0, slice(128, 256))]
                vs = [vb_(li)[:, 128 * (2 * g): 128 * (2 * g + 1)],
                      vb_(li)[:, 128 * (2 * g + 1): 128 * (2 * g + 2)]]
                group_attn(h, li, qb_(li)[:, qc], ks, vs, qc)

        p.wait("gpsimd", "C")
        ccv = d_ccout.ap().rearrange("(s w) p t -> s w p t", w=16)
        for h in range(8, 16):
            hh = h - 8
            p.wait("gpsimd", "P")
            p.dma("gpsimd", kpre.rearrange("p (s t) -> p s t", s=8),
                  ccv[:, 2 * hh, :, :].rearrange("s p t -> p s t"), "G")
            p.dma("gpsimd", vpre.rearrange("p (s t) -> p s t", s=8),
                  ccv[:, 2 * hh + 1, :, :].rearrange("s p t -> p s t"), "G")
            gl = p.cnt["G"]
            p.wait("tensor", "G", gl)
            qa = stash(hh, 0, 128)
            ks = [(kpre[:, 128 * s: 128 * (s + 1)], stripeB[:, s:s + 1], None, None) for s in range(8)]
            ks.append((stash(hh, 128, 256), None, 0, None))
            vs = [vpre[:, 128 * s: 128 * (s + 1)] for s in range(8)]
            vs.append(stash(hh, 256, 384))
            group_attn(h, h % 2, qa, ks, vs, slice(0, 128))

    p.join()

    if STAGE >= 5:
        emit_ln()
        p.join()
        emit_ffn()
        p.join()

    p.dma("gpsimd", d_out.rearrange("p (c t) -> p c t", c=NCH), x[:, :, :], "G")
    gfin = p.cnt["G"]
    p.wait("gpsimd", "G", gfin)

    with nc.Block() as block:
        @block.tensor
        def _(e):
            for f in p.ops["tensor"]:
                f(e)

        @block.vector
        def _(e):
            for f in p.ops["vector"]:
                f(e)

        @block.scalar
        def _(e):
            for f in p.ops["scalar"]:
                f(e)

        @block.gpsimd
        def _(e):
            for f in p.ops["gpsimd"]:
                f(e)

        @block.sync
        def _(e):
            for f in p.ops["sync"]:
                f(e)

    es.close()
    return nc


def _chunked_T(w):
    K, M = w.shape
    return np.ascontiguousarray(
        w.reshape(K // 128, 128, M).transpose(1, 0, 2).reshape(128, (K // 128) * M))


def stage_inputs(inputs):
    f32 = np.float32
    x0 = np.ascontiguousarray(inputs["L0_attn_a_hidden_states"], dtype=f32)
    Wq = inputs["L0_attn_a_Wq"]
    Wkva = inputs["L0_attn_a_Wkva"]
    kvnw_v = inputs["L0_attn_a_kv_a_norm_weight"]
    Wkvb = inputs["L0_attn_a_Wkvb"]
    Wo = inputs["L0_attn_a_Wo"]
    cos = np.asarray(inputs["L0_attn_a_cos"]).reshape(64)
    sin = np.asarray(inputs["L0_attn_a_sin"]).reshape(64)
    Wqb = inputs["L0_attn_b_Wq"]
    Wkb = inputs["L0_attn_b_Wk"]
    Wvb = inputs["L0_attn_b_Wv"]
    bq = inputs["L0_attn_b_bq"]
    bk = inputs["L0_attn_b_bk"]
    bv = inputs["L0_attn_b_bv"]
    gw = inputs["L0_ffn_gate_w"]
    uw = inputs["L0_ffn_up_w"]
    dw = inputs["L0_ffn_down_w"]
    gsc_v = inputs["L0_ffn_gate_scale"]
    usc_v = inputs["L0_ffn_up_scale"]
    dsc_v = inputs["L0_ffn_down_scale"]

    shared = {}
    WqT = np.ascontiguousarray(Wq.T)
    wqtn = np.empty((NH, 128, 2048), f32)
    wqpe = np.empty((NH, 64, 2048), f32)
    for h in range(NH):
        wqtn[h] = _chunked_T(WqT[:, 192 * h: 192 * h + 128])
        wqpe[h] = Wq[192 * h + 128: 192 * h + 192, :]
    shared["wqtn"] = np.ascontiguousarray(wqtn)
    shared["wqpe"] = np.ascontiguousarray(wqpe)
    WkvaT = np.ascontiguousarray(Wkva.T)
    shared["wkvat"] = _chunked_T(WkvaT[:, 0:512])
    shared["wkvape"] = np.ascontiguousarray(Wkva[512:576, :])
    WkvbT = np.ascontiguousarray(Wkvb.T)
    wkbk = np.empty((NH, 128, 512), f32)
    wkbv = np.empty((NH, 128, 512), f32)
    for h in range(NH):
        wkbk[h] = _chunked_T(WkvbT[:, 256 * h: 256 * h + 128])
        wkbv[h] = _chunked_T(WkvbT[:, 256 * h + 128: 256 * h + 256])
    shared["wkbk"] = np.ascontiguousarray(wkbk)
    shared["wkbv"] = np.ascontiguousarray(wkbv)
    WoT = np.ascontiguousarray(Wo.T)
    wot = np.empty((NCH, 128, 2048), f32)
    for fo in range(NCH):
        wot[fo] = _chunked_T(WoT[:, 128 * fo: 128 * (fo + 1)])
    shared["wot"] = np.ascontiguousarray(wot)
    for name, W in (("wqb", Wqb), ("wkb", Wkb), ("wvb", Wvb)):
        WT = np.ascontiguousarray(W.T)
        arr = np.empty((NH, 128, 2048), f32)
        for h in range(NH):
            arr[h] = _chunked_T(WT[:, 128 * h: 128 * (h + 1)])
        shared[name] = np.ascontiguousarray(arr)
    garr = np.empty((48, 128, 2048), f32)
    uarr = np.empty((48, 128, 2048), f32)
    for g in range(48):
        garr[g] = _chunked_T(gw[:, 128 * g: 128 * (g + 1)])
        uarr[g] = _chunked_T(uw[:, 128 * g: 128 * (g + 1)])
    shared["gw"] = np.ascontiguousarray(garr)
    shared["uw"] = np.ascontiguousarray(uarr)
    shared["dw"] = np.ascontiguousarray(dw.reshape(48, 128, 2048))
    shared["kvnwd"] = np.ascontiguousarray(np.asarray(kvnw_v, f32).reshape(4, 128).T)
    shared["cosd"] = np.ascontiguousarray(cos.reshape(64, 1).astype(f32))
    shared["sinrd"] = np.ascontiguousarray(np.roll(sin, -32).reshape(64, 1).astype(f32))
    shared["signvd"] = np.concatenate([np.ones(32, f32), -np.ones(32, f32)]).reshape(64, 1)
    shared["bqtd"] = np.ascontiguousarray(np.asarray(bq, f32).reshape(16, 128).T)
    shared["bktd"] = np.ascontiguousarray(np.asarray(bk, f32).reshape(16, 128).T)
    shared["bvrd"] = np.ascontiguousarray(np.asarray(bv, f32).reshape(1, 2048))
    shared["gscd"] = np.ascontiguousarray(np.asarray(gsc_v, f32).reshape(48, 128).T)
    shared["uscd"] = np.ascontiguousarray(np.asarray(usc_v, f32).reshape(48, 128).T)
    shared["dscd"] = np.ascontiguousarray(np.asarray(dsc_v, f32).reshape(16, 128).T)

    in_maps = []
    for c in range(NCORES):
        b, i = divmod(c, 4)
        x0T = x0[b].T
        rot = np.roll(x0T, -1024 * i, axis=1)
        x0kvr = np.ascontiguousarray(
            rot.reshape(NCH, 128, 4096).transpose(1, 0, 2).reshape(128, NCH * 4096))
        bias = np.full((128, 32), -CA, f32)
        for t in range(8, 32):
            g = (8 * i + t) % 32
            if g >= 8 * i:
                bias[:, t] = NEGINF
        stripe = np.full((128, 8), NEGINF, f32)
        pred = 4 * b + (i - 1) % 4
        stripe[:, pred] = -CB
        m = {"x0kvr": x0kvr, "biasA": bias, "stripeB": stripe}
        m.update(shared)
        in_maps.append(m)
    return in_maps


def kernel(**inputs):
    in_maps = stage_inputs(inputs)
    nc = build_program()
    res = run_bass_kernel_spmd(nc, in_maps, core_ids=list(range(NCORES)))
    out = np.empty((B, S, D), np.float32)
    for c in range(NCORES):
        b, i = divmod(c, 4)
        oT = res.results[c]["outT"]
        blk = oT.reshape(128, NCH, 1024).transpose(1, 0, 2).reshape(2048, 1024)
        out[b, 1024 * i: 1024 * (i + 1), :] = blk.T
    if DBG:
        kernel._dbg = [res.results[c].get("dbg") for c in range(NCORES)]
    return out


# version 02 (final)
# speedup vs baseline: 1.0775x; 1.0775x over previous
import os
import numpy as np
import ml_dtypes
from contextlib import ExitStack

import concourse.bass as bass
import concourse.mybir as mybir
from concourse.bass_utils import run_bass_kernel_spmd

F32 = mybir.dt.float32
BF16 = mybir.dt.bfloat16
F8 = mybir.dt.float8e4
AF = mybir.ActivationFunctionType
ALU = mybir.AluOpType

NCORES = 8
B, S, D = 2, 4096, 2048
T = 1024
NCH = 16
NH, DN, DR, DQ, DV, LORA = 16, 128, 64, 192, 128, 512
CA, CB = 25.0, 30.0
NEGINF = -30000.0
SCALE_A = float(192.0 ** -0.5)
SCALE_B = float(128.0 ** -0.5)
LN_EPS = 1e-5
RMS_EPS = 1e-6

STAGE = int(os.environ.get("MK_STAGE", "5"))
DBG = int(os.environ.get("MK_DBG", "0"))

BF = ml_dtypes.bfloat16


class PB:
    ENGINES = ("tensor", "vector", "scalar", "gpsimd", "sync")

    def __init__(self, nc, es):
        self.nc = nc
        self.es = es
        self.ops = {e: [] for e in self.ENGINES}
        self.sems = {}
        self.cnt = {}
        for ch in ("L", "G", "P", "V", "S", "Q", "C"):
            self.sems[ch] = es.enter_context(nc.semaphore("sem" + ch))
            self.cnt[ch] = 0

    def wait(self, eng, ch, val=None):
        v = self.cnt[ch] if val is None else val
        if v <= 0:
            return
        h = self.sems[ch]
        self.ops[eng].append(lambda e, h=h, v=v: e.wait_ge(h, v))

    def do(self, eng, fn, inc=None, n=1):
        if inc is None:
            self.ops[eng].append(fn)
        else:
            h = self.sems[inc]
            self.cnt[inc] += n
            self.ops[eng].append(lambda e, fn=fn, h=h, n=n: fn(e).then_inc(h, n))

    def dma(self, eng, out, in_, ch):
        self.do(eng, lambda e, o=out, i=in_: e.dma_start(out=o, in_=i), inc=ch, n=16)

    def join(self, engines=None, chans=None):
        if engines is None:
            engines = self.ENGINES
        if chans is None:
            chans = ("L", "G", "P", "V", "S", "Q")
        snap = {ch: self.cnt[ch] for ch in chans}
        for eng in engines:
            for ch in chans:
                self.wait(eng, ch, snap[ch])


def build_program():
    nc = bass.Bass("TRN2", target_bir_lowering=False, num_devices=NCORES,
                   dynamic_dma_scratch_size=8192)
    es = ExitStack()

    d_x0 = nc.dram_tensor("x0kvr", [128, NCH * 4096], F32, kind="ExternalInput").ap()
    d_biasA = nc.dram_tensor("biasA", [128, 32], F32, kind="ExternalInput").ap()
    d_stripeB = nc.dram_tensor("stripeB", [128, 8], F32, kind="ExternalInput").ap()
    d_wqtn = nc.dram_tensor("wqtn", [NH, 128, 2048], F32, kind="ExternalInput").ap()
    d_wqpe = nc.dram_tensor("wqpe", [NH, 64, 2048], F32, kind="ExternalInput").ap()
    d_wkvat = nc.dram_tensor("wkvat", [128, NCH * 512], F32, kind="ExternalInput").ap()
    d_wkvape = nc.dram_tensor("wkvape", [64, 2048], F32, kind="ExternalInput").ap()
    d_wkbk = nc.dram_tensor("wkbk", [NH, 128, 512], F32, kind="ExternalInput").ap()
    d_wkbv = nc.dram_tensor("wkbv", [NH, 128, 512], F32, kind="ExternalInput").ap()
    d_wot = nc.dram_tensor("wot", [NCH, 128, 2048], F32, kind="ExternalInput").ap()
    d_wqb = nc.dram_tensor("wqb", [NH, 128, 2048], F32, kind="ExternalInput").ap()
    d_wkb = nc.dram_tensor("wkb", [NH, 128, 2048], F32, kind="ExternalInput").ap()
    d_wvb = nc.dram_tensor("wvb", [NH, 128, 2048], F32, kind="ExternalInput").ap()
    d_gw = nc.dram_tensor("gw", [48, 128, 2048], F32, kind="ExternalInput").ap()
    d_uw = nc.dram_tensor("uw", [48, 128, 2048], F32, kind="ExternalInput").ap()
    d_dw = nc.dram_tensor("dw", [48, 128, 2048], F32, kind="ExternalInput").ap()
    d_kvnw = nc.dram_tensor("kvnwd", [128, 4], F32, kind="ExternalInput").ap()
    d_cos = nc.dram_tensor("cosd", [64, 1], F32, kind="ExternalInput").ap()
    d_sinr = nc.dram_tensor("sinrd", [64, 1], F32, kind="ExternalInput").ap()
    d_signv = nc.dram_tensor("signvd", [64, 1], F32, kind="ExternalInput").ap()
    d_bqt = nc.dram_tensor("bqtd", [128, 16], F32, kind="ExternalInput").ap()
    d_bkt = nc.dram_tensor("bktd", [128, 16], F32, kind="ExternalInput").ap()
    d_bvr = nc.dram_tensor("bvrd", [1, 2048], F32, kind="ExternalInput").ap()
    d_gsc = nc.dram_tensor("gscd", [128, 48], F32, kind="ExternalInput").ap()
    d_usc = nc.dram_tensor("uscd", [128, 48], F32, kind="ExternalInput").ap()
    d_dsc = nc.dram_tensor("dscd", [128, 16], F32, kind="ExternalInput").ap()
    d_qspn = nc.dram_tensor("qspn", [NH, 128, 1024], BF16)
    d_qspr = nc.dram_tensor("qspr", [NH, 64, 1024], BF16)
    d_ccin = nc.dram_tensor("ccin", [16, 128, 128], BF16)
    d_ccout = nc.dram_tensor("ccout", [NCORES * 16, 128, 128], BF16)
    d_out = nc.dram_tensor("outT", [128, NCH * 1024], F32, kind="ExternalOutput").ap()
    d_dbg = None
    if DBG:
        d_dbg = nc.dram_tensor("dbg", [128, 6 * 4096], F32, kind="ExternalOutput").ap()

    sb = lambda name, shape, dt: es.enter_context(nc.sbuf_tensor(name, shape, dt))
    sacrif = sb("sacrif", [128, 2048], F32)
    x = sb("x", [128, NCH, 1024], F32)
    actA = sb("actA", [128, NCH, 1024], BF16)
    ar2 = sb("ar2", [128, 16384], BF16)
    ar3 = sb("ar3", [128, 4096], BF16)
    ar4 = sb("ar4", [128, 4096], BF16)
    wpool = sb("wpool", [128, 8192], BF16)
    fland = sb("fland", [128, 4096], F32)
    f8a = sb("f8a", [128, 1024], F8)
    sq = sb("sq", [128, 1024], F32)
    sq2 = sb("sq2", [128, 1024], F32)
    bsb = sb("bsb", [128, 1024], F32)
    ptb = sb("ptb", [128, 2688], BF16)
    rowsA = sb("rowsA", [1, 2560], F32)
    mrow = rowsA[0:1, 0:512]
    m2row = rowsA[0:1, 512:1024]
    vrow = rowsA[0:1, 1024:1536]
    rrow = rowsA[0:1, 1536:2048]
    nrow = rowsA[0:1, 2048:2560]
    rlrow = rowsA[0:1, 0:512]
    rlrow2 = rowsA[0:1, 512:1024]
    bvrow = rowsA[0:1, 1920:2048]
    cbias = sb("cbias", [128, 4], F32)
    fscr = sb("fscr", [128, 4], F32)
    onesf = sb("onesf", [128, 1], F32)
    ones1f = sb("ones1f", [1, 128], F32)
    onesb = sb("onesb", [128, 1], BF16)
    trib = ptb[:, 2560:2688]
    rtb = sb("rtb", [64, 64], BF16)
    biasA = sb("biasA_sb", [128, 32], F32)
    stripeB = sb("stripeB_sb", [128, 8], F32)
    kvnw = sb("kvnw", [128, 4], F32)
    cosv = sb("cosv", [64, 1], F32)
    sinr = sb("sinr", [64, 1], F32)
    signv = sb("signv", [64, 1], F32)
    bqt = sb("bqt", [128, 16], F32)
    bkt = sb("bkt", [128, 16], F32)
    gsc = sb("gsc", [128, 48], F32)
    usc = sb("usc", [128, 48], F32)
    dsc = sb("dsc", [128, 16], F32)

    pk = [es.enter_context(nc.psum_tensor(f"pb{i}", [128, 512], F32)) for i in range(8)]

    p = PB(nc, es)

    ckvnl = lambda l: ar2[:, 4096 * l: 4096 * (l + 1)]
    ckvnr = ar3[0:64, 0:4096]
    wkvab = wpool[:, 0:8192]
    wkvafold = ar4[:, 0:1024]
    slab = lambda i: wpool[:, 2048 * i: 2048 * (i + 1)]
    land = lambda i: fland[:, 2048 * i: 2048 * (i + 1)]

    p.dma("sync", biasA[:, :], d_biasA, "L")
    p.dma("sync", stripeB[:, :], d_stripeB, "L")
    p.dma("sync", kvnw[:, :], d_kvnw, "L")
    p.dma("sync", cosv[:, :], d_cos, "L")
    p.dma("sync", sinr[:, :], d_sinr, "L")
    p.dma("sync", signv[:, :], d_signv, "L")
    p.dma("sync", bqt[:, :], d_bqt, "L")
    p.dma("sync", bkt[:, :], d_bkt, "L")
    p.dma("sync", gsc[:, :], d_gsc, "L")
    p.dma("sync", usc[:, :], d_usc, "L")
    p.dma("sync", dsc[:, :], d_dsc, "L")
    p.dma("sync", x[:, :, :], d_x0[:, 0: NCH * 4096].rearrange("p (c t) -> p c t", c=NCH)[:, :, 0:1024], "L")
    p.dma("sync", land(0)[:, :], d_wkvat[:, 0:2048], "L")
    p.dma("sync", land(1)[:, :], d_wkvat[:, 2048:4096], "L")
    ld_all = p.cnt["L"]

    p.do("vector", lambda e: e.memset(cbias[:, 0:1], LN_EPS), inc="V")
    p.do("vector", lambda e: e.memset(cbias[:, 1:2], RMS_EPS), inc="V")
    p.do("vector", lambda e: e.memset(cbias[:, 2:3], -CB), inc="V")
    p.do("vector", lambda e: e.memset(onesf[:, :], 1.0), inc="V")
    p.do("vector", lambda e: e.memset(ones1f[:, :], 1.0), inc="V")
    p.do("vector", lambda e: e.memset(onesb[:, :], 1.0), inc="V")


    t64a = sq2[0:64, 0:64]
    t64b = sq2[0:64, 64:128]
    t64c = sq2[0:64, 128:192]
    t64d = sq2[0:64, 192:256]
    t64o = sq2[0:64, 256:320]
    p.wait("vector", "L", ld_all)
    p.do("vector", lambda e: e.memset(t64o, 1.0), inc="V")
    p.do("vector", lambda e: e.tensor_scalar(t64a, t64o, cosv[:, :], None, ALU.mult), inc="V")
    p.do("vector", lambda e: e.tensor_scalar(t64b, t64o, sinr[:, :], signv[:, :], ALU.mult, ALU.mult), inc="V")
    p.wait("gpsimd", "V")
    p.do("gpsimd", lambda e: e.affine_select(
        out=t64a, in_=t64a, compare_op=ALU.is_equal, fill=0.0,
        base=0, pattern=[[-1, 64]], channel_multiplier=1), inc="Q")
    p.do("gpsimd", lambda e: e.affine_select(
        out=t64c, in_=t64b, compare_op=ALU.is_equal, fill=0.0,
        base=32, pattern=[[-1, 64]], channel_multiplier=1), inc="Q")
    p.do("gpsimd", lambda e: e.affine_select(
        out=t64d, in_=t64b, compare_op=ALU.is_equal, fill=0.0,
        base=-32, pattern=[[-1, 64]], channel_multiplier=1), inc="Q")
    p.wait("vector", "Q")
    p.do("vector", lambda e: e.tensor_tensor(t64a, t64a, t64c, ALU.add), inc="V")
    p.do("vector", lambda e: e.tensor_tensor(t64a, t64a, t64d, ALU.add), inc="V")
    p.do("vector", lambda e: e.tensor_copy(rtb[:, :], t64a), inc="V")

    p.wait("gpsimd", "L", ld_all)
    p.do("gpsimd", lambda e: e.tensor_copy(wkvab[:, 0:2048], land(0)[:, :]), inc="Q")
    p.do("gpsimd", lambda e: e.tensor_copy(wkvab[:, 2048:4096], land(1)[:, :]), inc="Q")
    v_land01_free = p.cnt["Q"]
    p.wait("sync", "Q", v_land01_free)
    p.dma("sync", land(0)[:, :], d_wkvat[:, 4096:6144], "L")
    p.dma("sync", land(1)[:, :], d_wkvat[:, 6144:8192], "L")
    ld2 = p.cnt["L"]
    p.wait("gpsimd", "L", ld2)
    p.do("gpsimd", lambda e: e.tensor_copy(wkvab[:, 4096:6144], land(0)[:, :]), inc="Q")
    p.do("gpsimd", lambda e: e.tensor_copy(wkvab[:, 6144:8192], land(1)[:, :]), inc="Q")
    p.wait("sync", "Q")
    p.dma("sync", land(0)[0:64, :], d_wkvape, "L")
    ld3 = p.cnt["L"]
    p.wait("gpsimd", "L", ld3)
    p.do("gpsimd", lambda e: e.tensor_copy(ar4[0:64, 1024:3072], land(0)[0:64, :]), inc="Q")
    p.wait("tensor", "Q")
    p.wait("tensor", "V")
    for c in range(NCH):
        if c >= 1:
            p.wait("tensor", "S")
        p.do("tensor", lambda e, c=c: e.matmul(
            pk[4][:, 0:64], ar4[0:64, 1024 + 128 * c:1024 + 128 * (c + 1)], rtb[:, :],
            start=True, stop=True), inc="P")
        p.wait("scalar", "P")
        p.do("scalar", lambda e, c=c: e.copy(wkvafold[:, 64 * c:64 * (c + 1)], pk[4][:, 0:64]), inc="S")

    p.join()

    xa = fland[:, 0:2048].rearrange("p (c t) -> p c t", c=NCH)
    for sbk in range(32):
        t0 = 128 * sbk
        p.wait("sync", "P")
        p.wait("sync", "V")
        p.wait("sync", "S")
        p.dma("sync", xa[:, :, :],
              d_x0.rearrange("p (c t) -> p c t", c=NCH)[:, :, t0:t0 + 128], "L")
        lA = p.cnt["L"]
        p.wait("scalar", "L", lA)
        p.wait("tensor", "L", lA)
        for c in range(NCH):
            sqs = sq[:, 128 * (c % 2): 128 * (c % 2) + 128]
            if c >= 2:
                p.wait("scalar", "P")
            p.do("scalar", lambda e, c=c, s=sqs: e.activation(s, xa[:, c, :], AF.Square), inc="S")
            p.wait("tensor", "S")
            p.do("tensor", lambda e, c=c: e.matmul(
                pk[0][0:1, 0:128], onesf[:, :], xa[:, c, :],
                start=(c == 0), stop=(c == NCH - 1)), inc="P")
            p.do("tensor", lambda e, c=c, s=sqs: e.matmul(
                pk[1][0:1, 0:128], onesf[:, :], s,
                start=(c == 0), stop=(c == NCH - 1)), inc="P")
        p.wait("vector", "P")
        p.do("vector", lambda e: e.tensor_scalar(mrow[:, 0:128], pk[0][0:1, 0:128], 1.0 / D, None, ALU.mult), inc="V")
        p.do("vector", lambda e: e.tensor_scalar(m2row[:, 0:128], pk[1][0:1, 0:128], 1.0 / D, None, ALU.mult), inc="V")
        p.do("vector", lambda e: e.tensor_tensor(vrow[:, 0:128], mrow[:, 0:128], mrow[:, 0:128], ALU.mult), inc="V")
        p.do("vector", lambda e: e.tensor_tensor(vrow[:, 0:128], m2row[:, 0:128], vrow[:, 0:128], ALU.subtract), inc="V")
        p.wait("scalar", "V")
        p.do("scalar", lambda e: e.activation(vrow[:, 0:128], vrow[:, 0:128], AF.Ln, bias=cbias[0:1, 0:1]), inc="S")
        p.do("scalar", lambda e: e.activation(rrow[:, 0:128], vrow[:, 0:128], AF.Exp, scale=-0.5), inc="S")
        p.wait("vector", "S")
        p.do("vector", lambda e: e.scalar_tensor_tensor(
            nrow[:, 0:128], mrow[:, 0:128], -1.0, rrow[:, 0:128], ALU.mult, ALU.mult), inc="V")
        p.wait("tensor", "V")
        p.do("tensor", lambda e: e.matmul(pk[2][:, 0:128], ones1f[:, :], rrow[:, 0:128], start=True, stop=True), inc="P")
        p.do("tensor", lambda e: e.matmul(pk[3][:, 0:128], ones1f[:, :], nrow[:, 0:128], start=True, stop=True), inc="P")
        pe_bc = p.cnt["P"]
        lnt = ptb[:, 0:2048].rearrange("p (c t) -> p c t", c=NCH)
        p.wait("vector", "P", pe_bc)
        for c in range(NCH):
            p.do("vector", lambda e, c=c: e.tensor_tensor(bsb[:, 0:128], xa[:, c, :], pk[2][:, 0:128], ALU.mult), inc="V")
            p.do("vector", lambda e, c=c: e.tensor_tensor(lnt[:, c, :], bsb[:, 0:128], pk[3][:, 0:128], ALU.add), inc="V")
            if sbk < 8:
                p.do("vector", lambda e, c=c, t0=t0: e.tensor_copy(actA[:, c, t0:t0 + 128], lnt[:, c, :]), inc="V")
        lnv = p.cnt["V"]
        lnsrc = lambda c: lnt[:, c, :]
        p.wait("tensor", "V", lnv)
        rms_pe = []
        for l in range(4):
            bank = pk[4 + (l % 2)]
            if l >= 2:
                p.wait("tensor", "S")
            for c in range(NCH):
                p.do("tensor", lambda e, l=l, c=c, b=bank: e.matmul(
                    b[:, 0:128], wkvab[:, 512 * c + 128 * l: 512 * c + 128 * (l + 1)], lnsrc(c),
                    start=(c == 0), stop=(c == NCH - 1)), inc="P")
            p.wait("scalar", "P")
            p.do("scalar", lambda e, l=l, b=bank: e.copy(sq2[:, 128 * ((l % 4)): 128 * (l % 4) + 128], b[:, 0:128]), inc="S")
            p.do("scalar", lambda e, l=l, b=bank: e.activation(sq[:, 128 * (l % 2): 128 * (l % 2) + 128], b[:, 0:128], AF.Square), inc="S")
            p.wait("tensor", "S")
            p.do("tensor", lambda e, l=l: e.matmul(
                pk[6][0:1, 0:128], onesf[:, :], sq[:, 128 * (l % 2): 128 * (l % 2) + 128],
                start=(l == 0), stop=(l == 3)), inc="P")
            rms_pe.append(p.cnt["P"])
        for c in range(NCH):
            p.do("tensor", lambda e, c=c: e.matmul(
                pk[7][0:64, 0:128], wkvafold[:, 64 * c: 64 * (c + 1)], lnsrc(c),
                start=(c == 0), stop=(c == NCH - 1)), inc="P")
        pe_rope = p.cnt["P"]
        p.wait("scalar", "P", pe_rope)
        p.do("scalar", lambda e, t0=t0: e.copy(ckvnr[:, t0:t0 + 128], pk[7][0:64, 0:128]), inc="S")
        p.wait("vector", "P", rms_pe[-1])
        p.do("vector", lambda e: e.tensor_scalar(vrow[:, 0:128], pk[6][0:1, 0:128], 1.0 / LORA, None, ALU.mult), inc="V")
        p.wait("scalar", "V")
        p.do("scalar", lambda e: e.activation(vrow[:, 0:128], vrow[:, 0:128], AF.Ln, bias=cbias[0:1, 1:2]), inc="S")
        p.do("scalar", lambda e: e.activation(rrow[:, 0:128], vrow[:, 0:128], AF.Exp, scale=-0.5), inc="S")
        p.wait("tensor", "S")
        p.do("tensor", lambda e: e.matmul(pk[2][:, 0:128], ones1f[:, :], rrow[:, 0:128], start=True, stop=True), inc="P")
        pe_rb = p.cnt["P"]
        p.wait("vector", "P", pe_rb)
        for l in range(4):
            p.do("vector", lambda e, l=l, t0=t0: e.scalar_tensor_tensor(
                ckvnl(l)[:, t0:t0 + 128], sq2[:, 128 * l: 128 * (l + 1)], kvnw[:, l:l + 1],
                pk[2][:, 0:128], ALU.mult, ALU.mult), inc="V")

    p.join()
    p.do("gpsimd", lambda e: e.memset(sq[:, 0:128], 1.0))
    p.do("gpsimd", lambda e: e.affine_select(
        out=sq[:, 0:128], in_=sq[:, 0:128], compare_op=ALU.is_ge, fill=0.0,
        base=0, pattern=[[1, 128]], channel_multiplier=-1), inc="Q")
    p.wait("vector", "Q")
    p.do("vector", lambda e: e.tensor_copy(trib[:, :], sq[:, 0:128]), inc="V")
    p.join()
    if DBG in (1, 2):
        for l in range(4):
            p.dma("gpsimd", d_dbg[:, 4096 * l: 4096 * (l + 1)],
                  ckvnl(l), "G")
        p.dma("gpsimd", d_dbg[0:64, 4 * 4096: 4 * 4096 + 4096], ckvnr, "G")
    if DBG in (1, 2):
        p.dma("gpsimd", d_dbg[:, 5 * 4096: 5 * 4096 + 1024], sq2[:, :], "G")
        p.dma("gpsimd", d_dbg[0:1, 5 * 4096 + 1024: 5 * 4096 + 1024 + 1024],
              rowsA[0:1, 1024:2048], "G")
    if DBG >= 2:
        for c in range(4):
            p.dma("gpsimd", d_dbg[:, 2 * 4096 + 1024 * c: 2 * 4096 + 1024 * (c + 1)],
                  actA[:, c, :], "G")

    if STAGE >= 2:
        for h in range(NH):
            li = h % 2
            p.wait("sync", "Q")
            p.dma("sync", land(li)[:, :], d_wqtn[h], "L")
            lq = p.cnt["L"]
            p.wait("gpsimd", "L", lq)
            p.do("gpsimd", lambda e, li=li: e.tensor_copy(slab(li)[:, :], land(li)[:, :]), inc="Q")
            qn_cast = p.cnt["Q"]
            p.wait("sync", "Q", qn_cast)
            p.dma("sync", land(li)[0:64, :], d_wqpe[h], "L")
            lqp = p.cnt["L"]
            p.wait("gpsimd", "L", lqp)
            p.do("gpsimd", lambda e, li=li: e.tensor_copy(slab(2 + li)[0:64, :], land(li)[0:64, :]), inc="Q")
            pe_cast = p.cnt["Q"]
            fold = ptb[:, 1024 * li: 1024 * li + 1024]
            p.wait("tensor", "Q", pe_cast)
            for c in range(NCH):
                p.wait("tensor", "S")
                p.do("tensor", lambda e, c=c, li=li: e.matmul(
                    pk[4][:, 0:64], slab(2 + li)[0:64, 128 * c:128 * (c + 1)], rtb[:, :],
                    start=True, stop=True), inc="P")
                p.wait("scalar", "P")
                p.do("scalar", lambda e, c=c, f=fold: e.copy(f[:, 64 * c: 64 * (c + 1)], pk[4][:, 0:64]), inc="S")
            fold_done = p.cnt["S"]
            qn_sb = ar4[:, 1024 * li: 1024 * li + 1024]
            qr_sb = ar4[0:64, 2048 + 1024 * li: 2048 + 1024 * li + 1024]
            p.wait("tensor", "S", fold_done)
            for half in range(2):
                for c in range(NCH):
                    p.do("tensor", lambda e, c=c, half=half, li=li: e.matmul(
                        pk[half][:, :], slab(li)[:, 128 * c:128 * (c + 1)],
                        actA[:, c, 512 * half: 512 * half + 512],
                        start=(c == 0), stop=(c == NCH - 1)), inc="P")
                for c in range(NCH):
                    p.do("tensor", lambda e, c=c, half=half, f=fold: e.matmul(
                        pk[2 + half][0:64, :], f[:, 64 * c: 64 * (c + 1)],
                        actA[:, c, 512 * half: 512 * half + 512],
                        start=(c == 0), stop=(c == NCH - 1)), inc="P")
            qpe_done = p.cnt["P"]
            p.wait("scalar", "P", qpe_done)
            for half in range(2):
                p.do("scalar", lambda e, half=half, q=qn_sb: e.copy(q[:, 512 * half:512 * half + 512], pk[half][:, :]), inc="S")
                p.do("scalar", lambda e, half=half, q=qr_sb: e.copy(q[0:64, 512 * half:512 * half + 512], pk[2 + half][0:64, :]), inc="S")
            qev = p.cnt["S"]
            p.wait("gpsimd", "S", qev)
            p.dma("gpsimd", d_qspn[h], qn_sb, "G")
            p.dma("gpsimd", d_qspr[h], qr_sb[0:64, :], "G")

    p.join()

    if STAGE >= 2:
        ksb = lambda i: ptb[:, 2048 + 128 * i: 2048 + 128 * (i + 1)]
        vsb = lambda i: ptb[:, 2304 + 128 * i: 2304 + 128 * (i + 1)]
        pt = lambda i: ptb[:, 1024 * i: 1024 * (i + 1)]
        lpart = sq2
        for h in range(NH):
            li = h % 2
            qn_sb = ar4[:, 1024 * li: 1024 * li + 1024]
            qr_sb = ar4[0:64, 2048 + 1024 * li: 2048 + 1024 * li + 1024]
            p.wait("gpsimd", "P")
            p.dma("gpsimd", qn_sb, d_qspn[h], "G")
            p.dma("gpsimd", qr_sb[0:64, :], d_qspr[h], "G")
            p.wait("sync", "V")
            p.dma("sync", land(li)[:, 0:512], d_wkbk[h], "L")
            p.dma("sync", land(li)[:, 512:1024], d_wkbv[h], "L")
            lw = p.cnt["L"]
            p.wait("gpsimd", "L", lw)
            wkh = wpool[:, 2048 * (li): 2048 * li + 512]
            wvh = wpool[:, 2048 * li + 512: 2048 * li + 1024]
            if DBG == 3 and h == 0:
                p.dma("gpsimd", d_dbg[:, 22528:23552], land(li)[:, 0:1024], "G")
            p.wait("vector", "L", lw)
            p.wait("vector", "P")
            p.do("vector", lambda e, w=wkh, li=li: e.tensor_copy(w, land(li)[:, 0:512]), inc="V")
            p.do("vector", lambda e, w=wvh, li=li: e.tensor_copy(w, land(li)[:, 512:1024]), inc="V")
            wcast = p.cnt["V"]
            qload = p.cnt["G"]
            p.wait("vector", "P")
            p.do("vector", lambda e: e.memset(lpart[:, :], 0.0), inc="V")
            if DBG == 3 and h == 0:
                p.wait("gpsimd", "V", wcast)
                p.dma("gpsimd", d_dbg[:, 21504:22016], wkh, "G")
                p.dma("gpsimd", d_dbg[:, 22016:22528], wvh, "G")
                gdump = p.cnt["G"]
                p.wait("tensor", "G", gdump)
                p.wait("sync", "G", gdump)
                p.wait("tensor", "G")
            p.wait("tensor", "G", qload)
            p.wait("tensor", "V", wcast)

            sc_evac = {}
            dv_evac = {}
            sc_exp = {}
            dv_mask = {}
            pe_S = {}
            pe_AV = {}

            def emit_kv(t, wkh=wkh, wvh=wvh):
                if t >= 1:
                    p.wait("tensor", "S", sc_evac.get(t - 1, 0))
                    p.wait("tensor", "V", dv_evac.get(t - 1, 0))
                for l in range(4):
                    p.do("tensor", lambda e, l=l, t=t, w=wkh: e.matmul(
                        pk[6][:, 0:128], w[:, 128 * l: 128 * (l + 1)],
                        ckvnl(l)[:, 128 * t: 128 * (t + 1)],
                        start=(l == 0), stop=(l == 3)), inc="P")
                for l in range(4):
                    p.do("tensor", lambda e, l=l, t=t, w=wvh: e.matmul(
                        pk[7][:, 0:128], ckvnl(l)[:, 128 * t: 128 * (t + 1)],
                        w[:, 128 * l: 128 * (l + 1)],
                        start=(l == 0), stop=(l == 3)), inc="P")
                kvp = p.cnt["P"]
                p.wait("scalar", "P", kvp)
                p.do("scalar", lambda e, t=t: e.copy(ksb(t % 2), pk[6][:, 0:128]), inc="S")
                p.wait("vector", "P", kvp)
                p.do("vector", lambda e, t=t: e.tensor_copy(vsb(t % 2), pk[7][:, 0:128]), inc="V")
                sc_evac[t] = p.cnt["S"]
                dv_evac[t] = p.cnt["V"]

            def emit_S(t, qn_sb=qn_sb, qr_sb=qr_sb):
                p.wait("tensor", "S", sc_evac[t])
                p.wait("tensor", "V", dv_evac[t])
                if t >= 2:
                    p.wait("tensor", "S", sc_exp[t - 2])
                for half in range(2):
                    bank = pk[2 + (t % 2) * 2 + half]
                    p.do("tensor", lambda e, t=t, half=half, b=bank, q=qn_sb: e.matmul(
                        b[:, :], ksb(t % 2), q[:, 512 * half: 512 * half + 512],
                        start=True, stop=False), inc="P")
                    p.do("tensor", lambda e, t=t, half=half, b=bank, q=qr_sb: e.matmul(
                        b[:, :], ckvnr[:, 128 * t: 128 * (t + 1)],
                        q[0:64, 512 * half: 512 * half + 512],
                        start=False, stop=True), inc="P")
                pe_S[t] = p.cnt["P"]
                p.wait("scalar", "P", pe_S[t])
                if t >= 2:
                    p.wait("scalar", "V", dv_mask[t - 1])
                    p.wait("scalar", "P", pe_AV.get(t - 2, 0))
                for half in range(2):
                    bank = pk[2 + (t % 2) * 2 + half]
                    p.do("scalar", lambda e, t=t, half=half, b=bank: e.activation(
                        pt(t % 2)[:, 512 * half: 512 * half + 512], b[:, :], AF.Exp,
                        bias=biasA[:, t:t + 1], scale=SCALE_A), inc="S")
                sc_exp[t] = p.cnt["S"]
                p.wait("vector", "S", sc_exp[t])
                if t < 8:
                    if t > 0:
                        p.do("vector", lambda e, t=t: e.memset(pt(t % 2)[:, 0:128 * t], 0.0), inc="V")
                    p.do("vector", lambda e, t=t: e.tensor_tensor(
                        pt(t % 2)[:, 128 * t: 128 * (t + 1)],
                        pt(t % 2)[:, 128 * t: 128 * (t + 1)], trib[:, :], ALU.mult), inc="V")
                if t % 2 == 1:
                    p.do("vector", lambda e: e.tensor_tensor(
                        bsb[:, :], pt(0), pt(1), ALU.add), inc="V")
                    p.do("vector", lambda e: e.tensor_tensor(lpart[:, :], lpart[:, :], bsb[:, :], ALU.add), inc="V")
                dv_mask[t] = p.cnt["V"]

            def emit_AV(t):
                p.wait("tensor", "S", sc_exp[t])
                p.wait("tensor", "V", dv_mask[t])
                for half in range(2):
                    p.do("tensor", lambda e, t=t, half=half: e.matmul(
                        pk[half][:, :], vsb(t % 2), pt(t % 2)[:, 512 * half: 512 * half + 512],
                        start=(t == 0), stop=(t == 31)), inc="P")
                pe_AV[t] = p.cnt["P"]

            emit_kv(0)
            if DBG == 5 and h in (0, 1):
                p.wait("gpsimd", "S", sc_evac[0])
                p.wait("gpsimd", "V", dv_evac[0])
                base = 19456 + h * 1024
                p.dma("gpsimd", d_dbg[:, base: base + 128], ksb(0), "G")
                p.dma("gpsimd", d_dbg[:, base + 128: base + 256], vsb(0), "G")
                p.wait("tensor", "G")
            for t in range(32):
                emit_S(t)
                if DBG == 5 and h in (0, 1) and t == 0:
                    p.wait("gpsimd", "V", dv_mask[0])
                    base = 19456 + h * 1024
                    p.dma("gpsimd", d_dbg[:, base + 256: base + 1024], pt(0)[:, 0:768], "G")
                    p.wait("tensor", "G")
                if t < 31:
                    emit_kv(t + 1)
                emit_AV(t)
            p.wait("tensor", "V")
            p.wait("tensor", "S")
            p.do("tensor", lambda e: e.matmul(pk[2][0:1, :], onesf[:, :], lpart[:, 0:512], start=True, stop=True), inc="P")
            p.do("tensor", lambda e: e.matmul(pk[4][0:1, :], onesf[:, :], lpart[:, 512:1024], start=True, stop=True), inc="P")
            lsum = p.cnt["P"]
            p.wait("vector", "P", lsum)
            p.do("vector", lambda e: e.reciprocal(rlrow, pk[2][0:1, :]), inc="V")
            p.do("vector", lambda e: e.reciprocal(rlrow2, pk[4][0:1, :]), inc="V")
            rec = p.cnt["V"]
            p.wait("tensor", "V", rec)
            p.do("tensor", lambda e: e.matmul(pk[3][:, :], ones1f[:, :], rlrow, start=True, stop=True), inc="P")
            p.do("tensor", lambda e: e.matmul(pk[5][:, :], ones1f[:, :], rlrow2, start=True, stop=True), inc="P")
            bc = p.cnt["P"]
            p.wait("scalar", "P", bc)
            p.do("scalar", lambda e: e.copy(bsb[:, 0:512], pk[3][:, :]))
            p.do("scalar", lambda e: e.copy(bsb[:, 512:1024], pk[5][:, :]))
            p.do("scalar", lambda e: e.copy(fscr[:, 0:1], cbias[:, 3:4]), inc="S")
            bcs = p.cnt["S"]
            if False and h == 0:
                p.do("scalar", lambda e: e.copy(sq[:, 0:512], pk[0][:, :]), inc="S")
                p.do("scalar", lambda e: e.copy(sq[:, 512:1024], pk[1][:, :]), inc="S")
                dbgs = p.cnt["S"]
                p.wait("gpsimd", "S", dbgs)
                p.wait("gpsimd", "V")
                p.dma("gpsimd", d_dbg[:, 19456:20480], sq[:, :], "G")
                p.dma("gpsimd", d_dbg[:, 20480:21504], sq2[:, :], "G")
                p.dma("gpsimd", d_dbg[0:1, 21504:21504 + 1152], rowsA[0:1, 2560:3712], "G")
                p.wait("scalar", "G")
                p.wait("tensor", "S", dbgs)
                p.wait("tensor", "G")
            p.wait("vector", "S", bcs)
            for half in range(2):
                p.do("vector", lambda e, h=h, half=half: e.tensor_tensor(
                    actA[:, h, 512 * half: 512 * half + 512], pk[half][:, :],
                    bsb[:, 512 * half: 512 * half + 512], ALU.mult), inc="V")
            p.wait("tensor", "V")

    p.join()
    if DBG == 3:
        for c in range(NCH):
            p.dma("gpsimd", d_dbg[:, 1024 * c: 1024 * (c + 1)], actA[:, c, :], "G")
        p.dma("gpsimd", d_dbg[:, 16384:17408], d_qspn[0], "G")
        p.dma("gpsimd", d_dbg[0:64, 17408:18432], d_qspr[0], "G")
        p.dma("gpsimd", d_dbg[:, 18432:19456], sq2[:, :], "G")
        p.join()

    if STAGE >= 2:
        cast_done = {}
        for fo in range(NCH):
            li = fo % 2
            if fo >= 2:
                p.wait("sync", "Q", cast_done[fo - 2])
            p.dma("sync", land(li)[:, :], d_wot[fo], "L")
            lw = p.cnt["L"]
            p.wait("gpsimd", "L", lw)
            p.wait("gpsimd", "P")
            p.do("gpsimd", lambda e, li=li: e.tensor_copy(slab(li)[:, :], land(li)[:, :]), inc="Q")
            cast_done[fo] = p.cnt["Q"]
            p.wait("tensor", "Q", cast_done[fo])
            if fo >= 2:
                p.wait("tensor", "V")
            for half in range(2):
                bank = pk[(fo % 2) * 2 + half]
                for ch in range(NCH):
                    p.do("tensor", lambda e, ch=ch, half=half, li=li, b=bank: e.matmul(
                        b[:, :], slab(li)[:, 128 * ch: 128 * (ch + 1)],
                        actA[:, ch, 512 * half: 512 * half + 512],
                        start=(ch == 0), stop=(ch == NCH - 1)), inc="P")
            wop = p.cnt["P"]
            p.wait("vector", "P", wop)
            for half in range(2):
                bank = pk[(fo % 2) * 2 + half]
                p.do("vector", lambda e, fo=fo, half=half, b=bank: e.tensor_tensor(
                    x[:, fo, 512 * half: 512 * half + 512],
                    x[:, fo, 512 * half: 512 * half + 512], b[:, :], ALU.add), inc="V")

    p.join()

    def emit_ln():
        for c in range(NCH):
            sqs = sq if c % 2 == 0 else bsb
            if c >= 2:
                p.wait("scalar", "P")
            p.do("scalar", lambda e, c=c, s=sqs: e.activation(s[:, :], x[:, c, :], AF.Square), inc="S")
            p.wait("tensor", "S")
            for half in range(2):
                p.do("tensor", lambda e, c=c, half=half: e.matmul(
                    pk[half][0:1, :], onesf[:, :], x[:, c, 512 * half: 512 * half + 512],
                    start=(c == 0), stop=(c == NCH - 1)), inc="P")
                p.do("tensor", lambda e, c=c, half=half, s=sqs: e.matmul(
                    pk[2 + half][0:1, :], onesf[:, :], s[:, 512 * half: 512 * half + 512],
                    start=(c == 0), stop=(c == NCH - 1)), inc="P")
        sums = p.cnt["P"]
        p.wait("vector", "P", sums)
        for half in range(2):
            if half == 1:
                p.wait("vector", "P")
            p.do("vector", lambda e, half=half: e.tensor_scalar(
                mrow, pk[half][0:1, :], 1.0 / D, None, ALU.mult), inc="V")
            p.do("vector", lambda e, half=half: e.tensor_scalar(
                m2row, pk[2 + half][0:1, :], 1.0 / D, None, ALU.mult), inc="V")
            p.do("vector", lambda e: e.tensor_tensor(vrow, mrow, mrow, ALU.mult), inc="V")
            p.do("vector", lambda e: e.tensor_tensor(vrow, m2row, vrow, ALU.subtract), inc="V")
            st = p.cnt["V"]
            p.wait("scalar", "V", st)
            p.do("scalar", lambda e: e.activation(vrow, vrow, AF.Ln, bias=cbias[0:1, 0:1]), inc="S")
            p.do("scalar", lambda e: e.activation(rrow, vrow, AF.Exp, scale=-0.5), inc="S")
            p.do("scalar", lambda e: e.copy(fscr[:, 0:1], cbias[:, 3:4]), inc="S")
            st2 = p.cnt["S"]
            p.wait("vector", "S", st2)
            p.do("vector", lambda e: e.scalar_tensor_tensor(
                nrow, mrow, -1.0, rrow, ALU.mult, ALU.mult), inc="V")
            st3 = p.cnt["V"]
            p.wait("tensor", "V", st3)
            p.do("tensor", lambda e, half=half: e.matmul(
                pk[4 + half][:, :], ones1f[:, :], rrow, start=True, stop=True), inc="P")
            p.do("tensor", lambda e, half=half: e.matmul(
                pk[6 + half][:, :], ones1f[:, :], nrow, start=True, stop=True), inc="P")
        bc = p.cnt["P"]
        p.wait("vector", "P", bc)
        for c in range(NCH):
            for half in range(2):
                hs = slice(512 * half, 512 * half + 512)
                p.do("vector", lambda e, c=c, half=half, hs=hs: e.tensor_tensor(
                    sq[:, 0:512], x[:, c, hs], pk[4 + half][:, :], ALU.mult), inc="V")
                p.do("vector", lambda e, c=c, half=half, hs=hs: e.tensor_tensor(
                    actA[:, c, hs], sq[:, 0:512], pk[6 + half][:, :], ALU.add), inc="V")

    def emit_ffn():
        fuse = lambda j: ar2[:, 1024 * j: 1024 * (j + 1)]
        dslab = lambda j: ar2[:, 4096 + 2048 * j: 4096 + 2048 * (j + 1)]
        def gsl(j):
            return ar2[:, 12288:14336] if j % 2 == 0 else ar4[:, 0:2048]

        def usl(j):
            return ar3[:, 0:2048] if j % 2 == 0 else ar4[:, 2048:4096]

        prev_consumed = {"g": {}, "u": {}, "d": {}}
        for sg in range(12):
            for j in range(4):
                g = 4 * sg + j
                li = g % 2
                p.wait("sync", "Q")
                p.dma("sync", land(li)[:, :], d_gw[g], "L")
                lg = p.cnt["L"]
                p.wait("gpsimd", "L", lg)
                p.wait("gpsimd", "P", prev_consumed["g"].get(j % 2, 0))
                p.do("gpsimd", lambda e, li=li: e.tensor_copy(f8a[:, :], land(li)[:, 0:1024]), inc="Q")
                p.do("gpsimd", lambda e, j=j: e.tensor_copy(gsl(j % 2)[:, 0:1024], f8a[:, :]), inc="Q")
                p.do("gpsimd", lambda e, li=li: e.tensor_copy(f8a[:, :], land(li)[:, 1024:2048]), inc="Q")
                p.do("gpsimd", lambda e, j=j: e.tensor_copy(gsl(j % 2)[:, 1024:2048], f8a[:, :]), inc="Q")
                gcast = p.cnt["Q"]
                p.wait("sync", "Q")
                p.wait("sync", "V")
                p.dma("sync", land(1 - li)[:, :], d_uw[g], "L")
                lu = p.cnt["L"]
                p.wait("vector", "L", lu)
                p.wait("vector", "P", prev_consumed["u"].get(j % 2, 0))
                f8v = f8a[:, 512:1024]
                for qtr in range(4):
                    p.do("vector", lambda e, li=li, f=f8v, q=qtr: e.tensor_copy(
                        f, land(1 - li)[:, 512 * q: 512 * (q + 1)]), inc="V")
                    p.do("vector", lambda e, j=j, f=f8v, q=qtr: e.tensor_copy(
                        usl(j % 2)[:, 512 * q: 512 * (q + 1)], f), inc="V")
                ucast = p.cnt["V"]
                p.wait("sync", "Q")
                p.dma("sync", land(li)[:, :], d_dw[g], "L")
                ldw = p.cnt["L"]
                p.wait("gpsimd", "L", ldw)
                p.wait("gpsimd", "P", prev_consumed["d"].get(j, 0))
                p.do("gpsimd", lambda e, li=li: e.tensor_copy(f8a[:, :], land(li)[:, 0:1024]), inc="Q")
                p.do("gpsimd", lambda e, j=j: e.tensor_copy(dslab(j)[:, 0:1024], f8a[:, :]), inc="Q")
                p.do("gpsimd", lambda e, li=li: e.tensor_copy(f8a[:, :], land(li)[:, 1024:2048]), inc="Q")
                p.do("gpsimd", lambda e, j=j: e.tensor_copy(dslab(j)[:, 1024:2048], f8a[:, :]), inc="Q")
                dcast = p.cnt["Q"]
                p.wait("tensor", "Q", gcast)
                p.wait("tensor", "S")
                p.wait("tensor", "V")
                for half in range(2):
                    for c in range(NCH):
                        p.do("tensor", lambda e, c=c, half=half, j=j: e.matmul(
                            pk[half][:, :], gsl(j % 2)[:, 128 * c: 128 * (c + 1)],
                            actA[:, c, 512 * half: 512 * half + 512],
                            start=(c == 0), stop=(c == NCH - 1)), inc="P")
                prev_consumed["g"][j % 2] = p.cnt["P"]
                gmm = p.cnt["P"]
                p.wait("scalar", "P", gmm)
                p.wait("scalar", "V")
                for half in range(2):
                    p.do("scalar", lambda e, half=half, g=g: e.activation(
                        sq[:, 512 * half: 512 * half + 512], pk[half][:, :], AF.Silu,
                        scale=gsc[:, g:g + 1]), inc="S")
                p.do("scalar", lambda e: e.copy(fscr[:, 0:1], cbias[:, 3:4]), inc="S")
                silu_done = p.cnt["S"]
                p.wait("tensor", "V")
                for half in range(2):
                    for c in range(NCH):
                        p.do("tensor", lambda e, c=c, half=half, j=j: e.matmul(
                            pk[2 + half][:, :], usl(j % 2)[:, 128 * c: 128 * (c + 1)],
                            actA[:, c, 512 * half: 512 * half + 512],
                            start=(c == 0), stop=(c == NCH - 1)), inc="P")
                prev_consumed["u"][j % 2] = p.cnt["P"]
                umm = p.cnt["P"]
                p.wait("vector", "P", umm)
                for half in range(2):
                    p.do("vector", lambda e, half=half, g=g: e.tensor_scalar(
                        sq2[:, 512 * half: 512 * half + 512], pk[2 + half][:, :],
                        usc[:, g:g + 1], None, ALU.mult), inc="V")
                p.wait("vector", "S", silu_done)
                p.do("vector", lambda e, j=j: e.tensor_tensor(fuse(j), sq[:, :], sq2[:, :], ALU.mult), inc="V")
                if DBG == 7 and g == 0:
                    fv = p.cnt["V"]
                    p.wait("gpsimd", "V", fv)
                    p.dma("gpsimd", d_dbg[:, 0:1024], sq[:, :], "G")
                    p.dma("gpsimd", d_dbg[:, 1024:2048], sq2[:, :], "G")
                    p.dma("gpsimd", d_dbg[:, 2048:3072], fuse(0), "G")
                    p.dma("gpsimd", d_dbg[:, 3072:5120], gsl(0), "G")
                    p.dma("gpsimd", d_dbg[:, 5120:7168], usl(0), "G")
                    gdmp = p.cnt["G"]
                    p.wait("vector", "G", gdmp)
                    p.wait("scalar", "G", gdmp)
                    p.wait("sync", "G", gdmp)
            fuse_done = p.cnt["V"]
            dcast_all = p.cnt["Q"]
            p.wait("tensor", "V", fuse_done)
            p.wait("tensor", "Q", dcast_all)
            dstt = {}
            for fo in range(NCH):
                pair = 4 + (fo % 2) * 2
                p.wait("tensor", "V", dstt.get(fo - 2, 0))
                for half in range(2):
                    bank = pk[pair + half]
                    for j in range(4):
                        p.do("tensor", lambda e, fo=fo, half=half, j=j, b=bank: e.matmul(
                            b[:, :], dslab(j)[:, 128 * fo: 128 * (fo + 1)],
                            fuse(j)[:, 512 * half: 512 * half + 512],
                            start=(j == 0), stop=(j == 3)), inc="P")
                dmm = p.cnt["P"]
                p.wait("vector", "P", dmm)
                for half in range(2):
                    bank = pk[pair + half]
                    p.do("vector", lambda e, fo=fo, half=half, b=bank: e.scalar_tensor_tensor(
                        x[:, fo, 512 * half: 512 * half + 512], b[:, :], dsc[:, fo:fo + 1],
                        x[:, fo, 512 * half: 512 * half + 512], ALU.mult, ALU.add), inc="V")
                dstt[fo] = p.cnt["V"]
                if DBG == 8 and sg == 0 and fo == 0:
                    p.wait("gpsimd", "V", dstt[fo])
                    p.dma("gpsimd", d_dbg[:, 0:1024], x[:, 0, :], "G")
                    for jj in range(4):
                        p.dma("gpsimd", d_dbg[:, 1024 + 1024 * jj: 2048 + 1024 * jj],
                              fuse(jj), "G")
                    p.dma("gpsimd", d_dbg[:, 5120:7168], dslab(0), "G")
                    gdmp = p.cnt["G"]
                    p.wait("tensor", "G", gdmp)
                    p.wait("vector", "G", gdmp)
            for j in range(4):
                prev_consumed["d"][j] = p.cnt["P"]

    if STAGE >= 3:
        emit_ln()
        p.join()
        if DBG == 6:
            for c in range(NCH):
                p.dma("gpsimd", d_dbg[:, 1024 * c: 1024 * (c + 1)], actA[:, c, :], "G")
            p.join()
        emit_ffn()
        p.join()

    if STAGE >= 4:
        emit_ln()
        p.join()
        qb_ = lambda i: ar2[:, 1024 * i: 1024 * (i + 1)]
        kb_ = lambda i: ar2[:, 2048 + 1024 * i: 2048 + 1024 * (i + 1)]
        vb_ = lambda i: ar2[:, 4096 + 1024 * i: 4096 + 1024 * (i + 1)]
        kpre = ar2[:, 6144:7168]
        vpre = ar2[:, 7168:8192]
        stash = lambda hh, lo, hi: ar2[:, 8192 + 384 * hh + lo: 8192 + 384 * hh + hi]

        def qkv_head(h):
            li = h % 2
            p.wait("sync", "Q")
            p.dma("sync", land(li)[:, :], d_wqb[h], "L")
            l1 = p.cnt["L"]
            p.wait("gpsimd", "L", l1)
            p.wait("gpsimd", "P")
            p.do("gpsimd", lambda e, li=li: e.tensor_copy(slab(li)[:, :], land(li)[:, :]), inc="Q")
            c1 = p.cnt["Q"]
            p.wait("tensor", "Q", c1)
            p.wait("tensor", "S")
            p.wait("tensor", "V")
            for half in range(2):
                for c in range(NCH):
                    p.do("tensor", lambda e, c=c, half=half, li=li: e.matmul(
                        pk[half][:, :], slab(li)[:, 128 * c: 128 * (c + 1)],
                        actA[:, c, 512 * half: 512 * half + 512],
                        start=(c == 0), stop=(c == NCH - 1)), inc="P")
            qmm = p.cnt["P"]
            p.wait("scalar", "P", qmm)
            for half in range(2):
                p.do("scalar", lambda e, half=half, h=h, li=li: e.activation(
                    qb_(li)[:, 512 * half: 512 * half + 512], pk[half][:, :], AF.Identity,
                    bias=bqt[:, h:h + 1]), inc="S")
            p.wait("sync", "Q")
            p.dma("sync", land(1 - li)[:, :], d_wkb[h], "L")
            l2 = p.cnt["L"]
            p.wait("gpsimd", "L", l2)
            p.do("gpsimd", lambda e, li=li: e.tensor_copy(slab(2 + li)[:, :], land(1 - li)[:, :]), inc="Q")
            c2 = p.cnt["Q"]
            p.wait("tensor", "Q", c2)
            for half in range(2):
                for c in range(NCH):
                    p.do("tensor", lambda e, c=c, half=half, li=li: e.matmul(
                        pk[2 + half][:, :], slab(2 + li)[:, 128 * c: 128 * (c + 1)],
                        actA[:, c, 512 * half: 512 * half + 512],
                        start=(c == 0), stop=(c == NCH - 1)), inc="P")
            kmm = p.cnt["P"]
            p.wait("scalar", "P", kmm)
            for half in range(2):
                p.do("scalar", lambda e, half=half, h=h, li=li: e.activation(
                    kb_(li)[:, 512 * half: 512 * half + 512], pk[2 + half][:, :], AF.Identity,
                    bias=bkt[:, h:h + 1]), inc="S")
            p.wait("sync", "P")
            p.dma("sync", bvrow, d_bvr[0:1, 128 * h: 128 * (h + 1)], "L")
            p.wait("sync", "Q")
            p.dma("sync", land(li)[:, :], d_wvb[h], "L")
            l3 = p.cnt["L"]
            p.wait("gpsimd", "L", l3)
            vsl = ar2[:, 11264 + 2048 * li: 11264 + 2048 * (li + 1)]
            p.do("gpsimd", lambda e, v=vsl, li=li: e.tensor_copy(v, land(li)[:, :]), inc="Q")
            c3 = p.cnt["Q"]
            p.wait("tensor", "Q", c3)
            for tc in range(8):
                if tc >= 1:
                    p.wait("tensor", "S")
                for c in range(NCH):
                    p.do("tensor", lambda e, c=c, tc=tc, v=vsl: e.matmul(
                        pk[4][:, 0:128], actA[:, c, 128 * tc: 128 * (tc + 1)],
                        v[:, 128 * c: 128 * (c + 1)],
                        start=(c == 0), stop=False), inc="P")
                p.do("tensor", lambda e, tc=tc: e.matmul(
                    pk[4][:, 0:128], ones1f[:, :], bvrow,
                    start=False, stop=True), inc="P")
                vmm = p.cnt["P"]
                p.wait("scalar", "P", vmm)
                p.do("scalar", lambda e, tc=tc, li=li: e.copy(
                    vb_(li)[:, 128 * tc: 128 * (tc + 1)], pk[4][:, 0:128]), inc="S")

        def group_attn(h, li, q0, k0chunks, v0chunks, xcols):
            Q = q0.shape[-1]
            nch_ = len(k0chunks)
            ptoff = 0
            ptslices = []
            for idx, (kap, bias_ap, tri_at, qsub) in enumerate(k0chunks):
                qs = q0 if qsub is None else q0[:, qsub]
                Qc = qs.shape[-1]
                bank = pk[5] if idx % 2 == 0 else pk[6]
                p.wait("tensor", "S")
                p.do("tensor", lambda e, kap=kap, qs=qs, b=bank, Qc=Qc: e.matmul(
                    b[:, 0:Qc], kap, qs, start=True, stop=True), inc="P")
                smm = p.cnt["P"]
                pts = ptb[:, ptoff: ptoff + Qc]
                ptslices.append((pts, qsub, Qc))
                ptoff += Qc
                p.wait("scalar", "P", smm)
                if bias_ap is None:
                    p.do("scalar", lambda e, b=bank, pts=pts, Qc=Qc: e.activation(
                        pts, b[:, 0:Qc], AF.Exp, bias=cbias[:, 2:3], scale=SCALE_B), inc="S")
                else:
                    p.do("scalar", lambda e, b=bank, pts=pts, Qc=Qc, ba=bias_ap: e.activation(
                        pts, b[:, 0:Qc], AF.Exp, bias=ba, scale=SCALE_B), inc="S")
                expd = p.cnt["S"]
                if tri_at is not None:
                    p.wait("vector", "S", expd)
                    p.do("vector", lambda e, pts=pts, ta=tri_at: e.tensor_tensor(
                        pts[:, ta:ta + 128], pts[:, ta:ta + 128], trib[:, :], ALU.mult), inc="V")
            maskd = p.cnt["V"]
            expall = p.cnt["S"]
            p.wait("tensor", "S", expall)
            p.wait("tensor", "V", maskd)
            for idx, (pts, qsub, Qc) in enumerate(ptslices):
                dst = pk[7][0:1, 0:Q] if qsub is None else pk[7][0:1, qsub]
                p.do("tensor", lambda e, pts=pts, d=dst, st=(idx == 0), sp=(idx == nch_ - 1): e.matmul(
                    d, onesb[:, :], pts, start=st, stop=sp), inc="P")
            for idx, ((pts, qsub, Qc), vap) in enumerate(zip(ptslices, v0chunks)):
                dst = pk[0][:, 0:Q] if qsub is None else pk[0][:, qsub]
                p.do("tensor", lambda e, pts=pts, d=dst, v=vap, st=(idx == 0), sp=(idx == nch_ - 1): e.matmul(
                    d, v, pts, start=st, stop=sp), inc="P")
            lav = p.cnt["P"]
            p.wait("vector", "P", lav)
            p.do("vector", lambda e, Q=Q: e.reciprocal(rlrow[:, 0:Q], pk[7][0:1, 0:Q]), inc="V")
            rc = p.cnt["V"]
            p.wait("tensor", "V", rc)
            p.wait("tensor", "S")
            p.do("tensor", lambda e, Q=Q: e.matmul(pk[4][:, 0:Q], ones1f[:, :], rlrow[:, 0:Q],
                                                   start=True, stop=True), inc="P")
            bc = p.cnt["P"]
            p.wait("scalar", "P", bc)
            p.do("scalar", lambda e, Q=Q: e.copy(bsb[:, 0:Q], pk[4][:, 0:Q]))
            p.do("scalar", lambda e: e.copy(fscr[:, 0:1], cbias[:, 3:4]), inc="S")
            bcs = p.cnt["S"]
            if DBG in (9, 11) and getattr(group_attn, "_dump", False):
                group_attn._dump = False
                p.do("scalar", lambda e, Q=Q: e.copy(sq2[:, 0:Q], pk[0][:, 0:Q]), inc="S")
                p.do("scalar", lambda e, Q=Q: e.copy(sq2[0:1, 512:512 + Q], pk[7][0:1, 0:Q]), inc="S")
                d9 = p.cnt["S"]
                p.wait("gpsimd", "S", d9)
                p.dma("gpsimd", d_dbg[:, 3072:4096], ptb[:, 0:1024], "G")
                p.dma("gpsimd", d_dbg[:, 4096:5120], sq2[:, 0:1024], "G")
                p.dma("gpsimd", d_dbg[0:1, 5120:5632], rlrow[:, :], "G")
                p.dma("gpsimd", d_dbg[:, 5632:6656], bsb[:, :], "G")
                gd9b = p.cnt["G"]
                p.wait("tensor", "G", gd9b)
                p.wait("vector", "G", gd9b)
                p.wait("scalar", "G", gd9b)
            p.wait("vector", "S", bcs)
            p.do("vector", lambda e, Q=Q: e.tensor_tensor(sq[:, 0:Q], pk[0][:, 0:Q], bsb[:, 0:Q], ALU.mult), inc="V")
            p.do("vector", lambda e: e.memset(fscr[:, 1:2], 0.0), inc="V")
            p.do("vector", lambda e, h=h, xc=xcols, Q=Q: e.tensor_tensor(
                x[:, h, xc], x[:, h, xc], sq[:, 0:Q], ALU.add), inc="V")
            p.wait("tensor", "V")

        for h in range(8, 16):
            li = h % 2
            qkv_head(h)
            qkvs = p.cnt["S"]
            if DBG == 9 and h == 8:
                p.wait("gpsimd", "S", qkvs)
                p.dma("gpsimd", d_dbg[:, 0:1024], qb_(li), "G")
                p.dma("gpsimd", d_dbg[:, 1024:2048], kb_(li), "G")
                p.dma("gpsimd", d_dbg[:, 2048:3072], vb_(li), "G")
                gd9 = p.cnt["G"]
                p.wait("tensor", "G", gd9)
                p.wait("vector", "G", gd9)
            p.wait("vector", "S", qkvs)
            hh = h - 8
            p.do("vector", lambda e, s=stash(hh, 0, 128), li=li: e.tensor_copy(s, qb_(li)[:, 0:128]), inc="V")
            p.do("vector", lambda e, s=stash(hh, 128, 256), li=li: e.tensor_copy(s, kb_(li)[:, 0:128]), inc="V")
            p.do("vector", lambda e, s=stash(hh, 256, 384), li=li: e.tensor_copy(s, vb_(li)[:, 0:128]), inc="V")
            p.wait("gpsimd", "S", qkvs)
            p.dma("gpsimd", d_ccin[2 * hh], kb_(li)[:, 896:1024], "G")
            p.dma("gpsimd", d_ccin[2 * hh + 1], vb_(li)[:, 896:1024], "G")
            if DBG == 9 and h == 8:
                group_attn._dump = True
            for g in range(3):
                qc = slice(128 + 256 * g, 128 + 256 * g + 256)
                ks = [(kb_(li)[:, 128 + 256 * g + 0:128 + 256 * g + 128], None, 0, None),
                      (kb_(li)[:, 128 + 256 * g + 128:128 + 256 * g + 256], None, 0, slice(128, 256))]
                vs = [vb_(li)[:, 128 * (2 * g + 1): 128 * (2 * g + 2)],
                      vb_(li)[:, 128 * (2 * g + 2): 128 * (2 * g + 3)]]
                group_attn(h, li, qb_(li)[:, qc], ks, vs, qc)
            qc = slice(896, 1024)
            ks = [(kb_(li)[:, 896:1024], None, 0, None)]
            vs = [vb_(li)[:, 896:1024]]
            group_attn(h, li, qb_(li)[:, qc], ks, vs, qc)

        p.wait("gpsimd", "G")
        p.do("gpsimd", lambda e: e.collective_compute(
            "AllGather", ALU.bypass,
            replica_groups=[list(range(NCORES))],
            ins=[d_ccin.ap()],
            outs=[d_ccout.ap()]), inc="C")

        for h in range(8):
            li = h % 2
            qkv_head(h)
            if DBG == 11 and h == 0:
                group_attn._dump = True
            if DBG == 10 and h == 0:
                d10 = p.cnt["S"]
                p.wait("gpsimd", "S", d10)
                p.dma("gpsimd", d_dbg[:, 0:1024], qb_(li), "G")
                p.dma("gpsimd", d_dbg[:, 1024:2048], kb_(li), "G")
                p.dma("gpsimd", d_dbg[:, 2048:3072], vb_(li), "G")
                gd10 = p.cnt["G"]
                p.wait("tensor", "G", gd10)
                p.wait("vector", "G", gd10)
            for g in range(4):
                qc = slice(256 * g, 256 * g + 256)
                ks = [(kb_(li)[:, 256 * g: 256 * g + 128], None, 0, None),
                      (kb_(li)[:, 256 * g + 128: 256 * g + 256], None, 0, slice(128, 256))]
                vs = [vb_(li)[:, 128 * (2 * g): 128 * (2 * g + 1)],
                      vb_(li)[:, 128 * (2 * g + 1): 128 * (2 * g + 2)]]
                group_attn(h, li, qb_(li)[:, qc], ks, vs, qc)

        p.wait("gpsimd", "C")
        ccv = d_ccout.ap().rearrange("(s w) p t -> s w p t", w=16)
        for h in range(8, 16):
            hh = h - 8
            p.wait("gpsimd", "P")
            p.dma("gpsimd", kpre.rearrange("p (s t) -> p s t", s=8),
                  ccv[:, 2 * hh, :, :].rearrange("s p t -> p s t"), "G")
            p.dma("gpsimd", vpre.rearrange("p (s t) -> p s t", s=8),
                  ccv[:, 2 * hh + 1, :, :].rearrange("s p t -> p s t"), "G")
            gl = p.cnt["G"]
            p.wait("tensor", "G", gl)
            qa = stash(hh, 0, 128)
            ks = [(kpre[:, 128 * s: 128 * (s + 1)], stripeB[:, s:s + 1], None, None) for s in range(8)]
            ks.append((stash(hh, 128, 256), None, 0, None))
            vs = [vpre[:, 128 * s: 128 * (s + 1)] for s in range(8)]
            vs.append(stash(hh, 256, 384))
            group_attn(h, h % 2, qa, ks, vs, slice(0, 128))

    p.join()

    if STAGE >= 5:
        emit_ln()
        p.join()
        emit_ffn()
        p.join()

    p.dma("gpsimd", d_out.rearrange("p (c t) -> p c t", c=NCH), x[:, :, :], "G")
    gfin = p.cnt["G"]
    p.wait("gpsimd", "G", gfin)

    with nc.Block() as block:
        @block.tensor
        def _(e):
            for f in p.ops["tensor"]:
                f(e)

        @block.vector
        def _(e):
            for f in p.ops["vector"]:
                f(e)

        @block.scalar
        def _(e):
            for f in p.ops["scalar"]:
                f(e)

        @block.gpsimd
        def _(e):
            for f in p.ops["gpsimd"]:
                f(e)

        @block.sync
        def _(e):
            for f in p.ops["sync"]:
                f(e)

    es.close()
    return nc


def _chunked_T(w):
    K, M = w.shape
    return np.ascontiguousarray(
        w.reshape(K // 128, 128, M).transpose(1, 0, 2).reshape(128, (K // 128) * M))


def stage_inputs(inputs):
    f32 = np.float32
    x0 = np.ascontiguousarray(inputs["L0_attn_a_hidden_states"], dtype=f32)
    Wq = inputs["L0_attn_a_Wq"]
    Wkva = inputs["L0_attn_a_Wkva"]
    kvnw_v = inputs["L0_attn_a_kv_a_norm_weight"]
    Wkvb = inputs["L0_attn_a_Wkvb"]
    Wo = inputs["L0_attn_a_Wo"]
    cos = np.asarray(inputs["L0_attn_a_cos"]).reshape(64)
    sin = np.asarray(inputs["L0_attn_a_sin"]).reshape(64)
    Wqb = inputs["L0_attn_b_Wq"]
    Wkb = inputs["L0_attn_b_Wk"]
    Wvb = inputs["L0_attn_b_Wv"]
    bq = inputs["L0_attn_b_bq"]
    bk = inputs["L0_attn_b_bk"]
    bv = inputs["L0_attn_b_bv"]
    gw = inputs["L0_ffn_gate_w"]
    uw = inputs["L0_ffn_up_w"]
    dw = inputs["L0_ffn_down_w"]
    gsc_v = inputs["L0_ffn_gate_scale"]
    usc_v = inputs["L0_ffn_up_scale"]
    dsc_v = inputs["L0_ffn_down_scale"]

    shared = {}
    WqT = np.ascontiguousarray(Wq.T)
    wqtn = np.empty((NH, 128, 2048), f32)
    wqpe = np.empty((NH, 64, 2048), f32)
    for h in range(NH):
        wqtn[h] = _chunked_T(WqT[:, 192 * h: 192 * h + 128])
        wqpe[h] = Wq[192 * h + 128: 192 * h + 192, :]
    shared["wqtn"] = np.ascontiguousarray(wqtn)
    shared["wqpe"] = np.ascontiguousarray(wqpe)
    WkvaT = np.ascontiguousarray(Wkva.T)
    shared["wkvat"] = _chunked_T(WkvaT[:, 0:512])
    shared["wkvape"] = np.ascontiguousarray(Wkva[512:576, :])
    WkvbT = np.ascontiguousarray(Wkvb.T)
    wkbk = np.empty((NH, 128, 512), f32)
    wkbv = np.empty((NH, 128, 512), f32)
    for h in range(NH):
        wkbk[h] = _chunked_T(WkvbT[:, 256 * h: 256 * h + 128])
        wkbv[h] = _chunked_T(WkvbT[:, 256 * h + 128: 256 * h + 256])
    shared["wkbk"] = np.ascontiguousarray(wkbk)
    shared["wkbv"] = np.ascontiguousarray(wkbv)
    WoT = np.ascontiguousarray(Wo.T)
    wot = np.empty((NCH, 128, 2048), f32)
    for fo in range(NCH):
        wot[fo] = _chunked_T(WoT[:, 128 * fo: 128 * (fo + 1)])
    shared["wot"] = np.ascontiguousarray(wot)
    for name, W in (("wqb", Wqb), ("wkb", Wkb), ("wvb", Wvb)):
        WT = np.ascontiguousarray(W.T)
        arr = np.empty((NH, 128, 2048), f32)
        for h in range(NH):
            arr[h] = _chunked_T(WT[:, 128 * h: 128 * (h + 1)])
        shared[name] = np.ascontiguousarray(arr)
    garr = np.empty((48, 128, 2048), f32)
    uarr = np.empty((48, 128, 2048), f32)
    for g in range(48):
        garr[g] = _chunked_T(gw[:, 128 * g: 128 * (g + 1)])
        uarr[g] = _chunked_T(uw[:, 128 * g: 128 * (g + 1)])
    shared["gw"] = np.ascontiguousarray(garr)
    shared["uw"] = np.ascontiguousarray(uarr)
    shared["dw"] = np.ascontiguousarray(dw.reshape(48, 128, 2048))
    shared["kvnwd"] = np.ascontiguousarray(np.asarray(kvnw_v, f32).reshape(4, 128).T)
    shared["cosd"] = np.ascontiguousarray(cos.reshape(64, 1).astype(f32))
    shared["sinrd"] = np.ascontiguousarray(np.roll(sin, -32).reshape(64, 1).astype(f32))
    shared["signvd"] = np.concatenate([np.ones(32, f32), -np.ones(32, f32)]).reshape(64, 1)
    shared["bqtd"] = np.ascontiguousarray(np.asarray(bq, f32).reshape(16, 128).T)
    shared["bktd"] = np.ascontiguousarray(np.asarray(bk, f32).reshape(16, 128).T)
    shared["bvrd"] = np.ascontiguousarray(np.asarray(bv, f32).reshape(1, 2048))
    shared["gscd"] = np.ascontiguousarray(np.asarray(gsc_v, f32).reshape(48, 128).T)
    shared["uscd"] = np.ascontiguousarray(np.asarray(usc_v, f32).reshape(48, 128).T)
    shared["dscd"] = np.ascontiguousarray(np.asarray(dsc_v, f32).reshape(16, 128).T)

    in_maps = []
    for c in range(NCORES):
        b, i = divmod(c, 4)
        x0T = x0[b].T
        rot = np.roll(x0T, -1024 * i, axis=1)
        x0kvr = np.ascontiguousarray(
            rot.reshape(NCH, 128, 4096).transpose(1, 0, 2).reshape(128, NCH * 4096))
        bias = np.full((128, 32), -CA, f32)
        for t in range(8, 32):
            g = (8 * i + t) % 32
            if g >= 8 * i:
                bias[:, t] = NEGINF
        stripe = np.full((128, 8), NEGINF, f32)
        pred = 4 * b + (i - 1) % 4
        stripe[:, pred] = -CB
        m = {"x0kvr": x0kvr, "biasA": bias, "stripeB": stripe}
        m.update(shared)
        in_maps.append(m)
    return in_maps


def kernel(**inputs):
    in_maps = stage_inputs(inputs)
    nc = build_program()
    res = run_bass_kernel_spmd(nc, in_maps, core_ids=list(range(NCORES)))
    out = np.empty((B, S, D), np.float32)
    for c in range(NCORES):
        b, i = divmod(c, 4)
        oT = res.results[c]["outT"]
        blk = oT.reshape(128, NCH, 1024).transpose(1, 0, 2).reshape(2048, 1024)
        out[b, 1024 * i: 1024 * (i + 1), :] = blk.T
    if DBG:
        kernel._dbg = [res.results[c].get("dbg") for c in range(NCORES)]
    return out
